# version 07
# baseline (speedup 1.0000x reference)
import numpy as np
from contextlib import ExitStack

import concourse.bass as bass
import concourse.mybir as mybir
from concourse.bass_utils import run_bass_kernel_spmd

F32 = mybir.dt.float32
BF16 = mybir.dt.bfloat16
I32 = mybir.dt.int32
AF = mybir.ActivationFunctionType
ALU = mybir.AluOpType
AX = mybir.AxisListType.X

P = 128
T = 2048
D = 1024
FF = 4096
C1 = 256
NC1 = T // C1
C2 = 128
NC2 = T // C2
EPS_LN = 1e-5
EPS_PHI = 1e-6
EPS_DEN = 1e-6

WQTH, WQTL = 0, 8192
WKTH, WKTL = 16384, 24576
WVTH, WVTL = 32768, 40960
WOTH, WOTL = 49152, 57344
W1AH, W1AL = 0, 16384
W2AH, W2AL = 32768, 49152

A1_XB = [0, 2048]
A1_XN = 4096
A1_XNTH = 12288
A1_XNTL = 14336
A1_QF = 8192
A1_KF = 10240
A1_V = 12288
A1_TMP = 14336
A1_WST = [24576, 25600]

A2_QF = [1024, 5120]
A2_KF = [9216, 13312]
A2_VA = [17408, 19488]
A2_KFT = 21568
A2_A = 0
A2_NUMR = 6144
A2_NUMRT = 7168
A2_S = [8192, 10240]
A2_DEN = 12288
A2_KVS = [12320, 13632]
A2_WST = [0, 0]

F_X = 0
F_XN = 2048
F_LN2 = 4096
F_LN2TH = 12288
F_LN2TL = 14336
F_YH = 16384
F_YL = 20480
F_S = 12288
F_ZP = 14336
F_OUT = 4096
F_DEQ = [8192, 10240]
F_SREP = [12288, 14336]
F_IC2 = [12288, 13312]
F_SR2 = [4096, 5120]

NUM_PS = 2048


def num_off(h):
    if h < 14:
        return 512 * (h // 7) + 65 * (h % 7)
    return 1024 + 65 * (h - 14)


def dkv_col(h):
    if h < 7:
        return 65 * h
    if h < 14:
        return 512 + 65 * (h - 7)
    return 3584 + 65 * (h - 14)


def build_nc(debug=False, max_phase=4):
    MP = max_phase
    nc = bass.Bass("TRN2", target_bir_lowering=False)

    x_in = nc.dram_tensor("x_in", [T, D], F32, kind="ExternalInput").ap()
    wq = nc.dram_tensor("wq", [D, D], F32, kind="ExternalInput").ap()
    wk = nc.dram_tensor("wk", [D, D], F32, kind="ExternalInput").ap()
    wv = nc.dram_tensor("wv", [D, D], F32, kind="ExternalInput").ap()
    wo = nc.dram_tensor("wo", [D, D], F32, kind="ExternalInput").ap()
    fc1q = nc.dram_tensor("fc1q", [D, FF], I32, kind="ExternalInput").ap()
    fc2q = nc.dram_tensor("fc2q", [FF, D], I32, kind="ExternalInput").ap()
    s1 = nc.dram_tensor("s1", [8, FF], F32, kind="ExternalInput").ap()
    s2 = nc.dram_tensor("s2", [32, D], F32, kind="ExternalInput").ap()
    b1 = nc.dram_tensor("b1", [P, 32], F32, kind="ExternalInput").ap()
    b2 = nc.dram_tensor("b2", [1, D], F32, kind="ExternalInput").ap()
    lng = nc.dram_tensor("lng", [1, D], F32, kind="ExternalInput").ap()
    lnb = nc.dram_tensor("lnb", [1, D], F32, kind="ExternalInput").ap()
    identf = nc.dram_tensor("identf", [P, P], F32, kind="ExternalInput").ap()
    maskw = nc.dram_tensor("maskw", [P, 384], BF16, kind="ExternalInput").ap()
    out = nc.dram_tensor("out", [T, D], F32, kind="ExternalOutput").ap()

    ksc = "ExternalOutput" if debug else "Internal"
    qf_sc = nc.dram_tensor("qf_sc", [NC1, P, 2048], F32, kind=ksc).ap()
    kf_sc = nc.dram_tensor("kf_sc", [NC1, P, 2048], F32, kind=ksc).ap()
    v_sc = nc.dram_tensor("v_sc", [NC1, 2, P, D], F32, kind=ksc).ap()
    s_sc = nc.dram_tensor("s_sc", [NC2, P, D], F32, kind=ksc).ap()
    z2p_sc = nc.dram_tensor("z2p_sc", [NC1, 2, P, D], F32, kind="Internal").ap()

    es = ExitStack()
    sbuf = lambda nm, shape, dt=F32: es.enter_context(nc.sbuf_tensor(nm, shape, dt))
    sem = lambda nm: es.enter_context(nc.semaphore(nm))

    with es:
        WA = sbuf("WA", [P, 32768])
        AR = sbuf("AR", [P, 16384])
        ident_s = sbuf("ident_s", [P, P])
        maskw_s = sbuf("maskw_s", [P, 384], BF16)
        stats = sbuf("stats", [P, 72])
        grep = sbuf("grep", [P, D])
        brep = sbuf("brep", [P, D])
        b2rep = sbuf("b2rep", [P, D])
        b1_s = sbuf("b1_s", [P, 32])
        kvs = None
        PS = es.enter_context(nc.psum_tensor("PS", [P, 4096], F32))
        kvsb = [AR[0:64, A2_KVS[i]:A2_KVS[i] + 1040] for i in range(2)]
        WB = WA[:].bitcast(BF16)
        ARB = AR[:].bitcast(BF16)
        YTMP = sbuf("YTMP", [128, 512])

        def wb(off, n):
            return WB[:, off:off + n]

        def arb(off, n):
            return ARB[:, off:off + n]

        def mm3(tensor, out, lh, ll, rh, rl, start, stop):
            tensor.matmul(out, lh, rh, start=start, stop=False)
            tensor.matmul(out, lh, rl, start=False, stop=False)
            return tensor.matmul(out, ll, rh, start=False, stop=stop)

        def ar(off, n):
            return AR[:, off:off + n]

        def bk(b, off=0, n=512):
            return PS[:, 512 * b + off:512 * b + off + n]

        def ps_strided(base, step, count, width):
            return PS[:, base:base + step * count].rearrange(
                "r (c w) -> r c w", w=step)[:, :, 0:width]

        class DrainAfter:
            def __init__(self, eng):
                self._e = eng

            def __getattr__(self, name):
                f = getattr(self._e, name)
                if name in ("wait_ge", "drain", "nop", "engine_nop", "dma_start"):
                    return f
                def g(*a, **k):
                    r = f(*a, **k)
                    self._e.drain()
                    return r
                return g

        ldc = sem("ldc")
        ldcg = sem("ldcg")
        phb = [sem("phb%d" % i) for i in range(3)]
        PHB = 5
        a1_wld = [sem("a1_wld0"), sem("a1_wld1")]
        a1_tr = sem("a1_tr")
        a1_wev = sem("a1_wev")
        a1_xev = sem("a1_xev")
        a1_ldx = [sem("a1_ldx0"), sem("a1_ldx1")]
        a1_xc = sem("a1_xc")
        a1_vr = sem("a1_vr")
        a1_sq = sem("a1_sq")
        a1_xnr = sem("a1_xnr")
        a1_qmq = sem("a1_qmq")
        a1_qeq = sem("a1_qeq")
        a1_qmk = sem("a1_qmk")
        a1_qek = sem("a1_qek")
        a1_vmm = sem("a1_vmm")
        a1_ve = sem("a1_ve")
        a1_qr = sem("a1_qr")
        st_q = sem("st_q")
        st_k = sem("st_k")
        st_v = sem("st_v")
        a2_wld = [sem("a2_wld0"), sem("a2_wld1")]
        a2_tr = sem("a2_tr")
        a2_wev = sem("a2_wev")
        a2_ktre = sem("a2_ktre")
        a2_ld = [sem("a2_ld0"), sem("a2_ld1")]
        a2_cons = sem("a2_cons")
        a2_amm = sem("a2_amm")
        a2_aev = sem("a2_aev")
        a2_nmm = sem("a2_nmm")
        a2_nev = sem("a2_nev")
        a2_ntr = sem("a2_ntr")
        a2_ntre = sem("a2_ntre")
        a2_smm = sem("a2_smm")
        a2_sev = sem("a2_sev")
        a2_sst = [sem("a2_sst0"), sem("a2_sst1")]
        a2_kvu = sem("a2_kvu")
        f_dld = [[sem("f_dld%d_%d" % (i, j)) for j in range(2)] for i in range(2)]
        f_dls = [[sem("f_dls%d_%d" % (i, j)) for j in range(2)] for i in range(2)]
        f_deq = [sem("f_deq%d" % i) for i in range(2)]
        f_ldx = [sem("f_ldx%d" % i) for i in range(2)]
        f_ldz = sem("f_ldz")
        f_xc = [sem("f_xc%d" % i) for i in range(2)]
        f_lnr = [sem("f_lnr%d" % i) for i in range(2)]
        f_tr = [sem("f_tr%d" % i) for i in range(2)]
        f_tre = [sem("f_tre%d" % i) for i in range(2)]
        f_ymm = [sem("f_ymm%d" % i) for i in range(2)]
        f_yev = [sem("f_yev%d" % i) for i in range(2)]
        f_zmm = [sem("f_zmm%d" % i) for i in range(2)]
        f_zev = [sem("f_zev%d" % i) for i in range(2)]
        f_ysh = [sem("f_ysh%d" % i) for i in range(2)]
        f_gv = sem("f_gv")
        f_p5 = sem("f_p5")
        f_og0 = sem("f_og0")
        x2o = sem("x2o")
        a2_dkv = sem("a2_dkv")
        f_gs = sem("f_gs")
        f_st = [sem("f_st%d" % i) for i in range(2)]

        block = es.enter_context(nc.Block())

        @block.sync
        def _(sync):
            sync.dma_start(out=ident_s[:], in_=identf).then_inc(ldc, 16)
            sync.dma_start(out=maskw_s[:], in_=maskw).then_inc(ldc, 16)
            sync.dma_start(out=b1_s[:], in_=b1).then_inc(ldc, 16)
            for c in range(2):
                sync.dma_start(
                    out=ar(A1_XB[c % 2], 2048).rearrange("p (s d) -> p s d", s=2),
                    in_=x_in[c * C1:(c + 1) * C1, :].rearrange("(s p) d -> p s d", p=P)
                ).then_inc(a1_ldx[c % 2], 16)
            for c in range(NC1):
                sync.wait_ge(a1_qr, 2 * c + 1)
                sync.dma_start(out=qf_sc[c], in_=ar(A1_QF, 2048)).then_inc(st_q, 16)
                sync.wait_ge(a1_qr, 2 * c + 2)
                sync.dma_start(out=kf_sc[c], in_=ar(A1_KF, 2048)).then_inc(st_k, 16)
                sync.wait_ge(a1_ve, 4 * c + 4)
                inst = sync.dma_start(out=v_sc[c].rearrange("s p d -> p s d"),
                                      in_=ar(A1_V, 2048).rearrange("p (s d) -> p s d", s=2))
                inst.then_inc(st_v, 16)
                if c + 2 < NC1:
                    sync.wait_ge(a1_xc, c + 1)
                    sync.dma_start(
                        out=ar(A1_XB[c % 2], 2048).rearrange("p (s d) -> p s d", s=2),
                        in_=x_in[(c + 2) * C1:(c + 3) * C1, :]
                        .rearrange("(s p) d -> p s d", p=P)
                    ).then_inc(a1_ldx[c % 2], 16)
            sync.wait_ge(st_q, 16 * NC1)
            sync.wait_ge(st_k, 16 * NC1)
            sync.wait_ge(st_v, 16 * NC1)
            sync.nop().then_inc(phb[0], 1)
            sync.wait_ge(phb[0], PHB)
            for c in range(8 if MP >= 2 else 0):
                if c >= 2:
                    sync.wait_ge(a2_cons, c - 1)
                sl = c % 2
                for par in range(2):
                    sync.dma_start(
                        out=WA[0:64, A2_QF[sl]:A2_QF[sl] + 4096]
                        .rearrange("r (p two t) -> r p two t", two=2, t=256)
                        [:, :, par, :],
                        in_=qf_sc[c][par * 64:par * 64 + 64]
                        .rearrange("r (p t) -> r p t", p=8)
                    ).then_inc(a2_ld[sl], 16)
                    sync.dma_start(
                        out=WA[0:64, A2_KF[sl]:A2_KF[sl] + 4096]
                        .rearrange("r (p two t) -> r p two t", two=2, t=256)
                        [:, :, par, :],
                        in_=kf_sc[c][par * 64:par * 64 + 64]
                        .rearrange("r (p t) -> r p t", p=8)
                    ).then_inc(a2_ld[sl], 16)
                for sub in range(2):
                    sync.dma_start(
                        out=WA[:, A2_VA[sl] + sub * 1040:A2_VA[sl] + sub * 1040 + 1040]
                        .rearrange("r (h e) -> r h e", e=65)[:, :, 0:64],
                        in_=v_sc[c, sub].rearrange("p (h e) -> p h e", e=64)
                    ).then_inc(a2_ld[sl], 16)
                for ts_ in range(2):
                    sync.wait_ge(a2_sev, 2 * c + ts_ + 1)
                    inst = sync.dma_start(out=s_sc[2 * c + ts_],
                                          in_=ar(A2_S[sl] + ts_ * 1024, 1024))
                    inst.then_inc(a2_sst[sl], 16)
            sync.wait_ge(a2_sst[0], 16 * 8 if MP >= 2 else 0)
            sync.wait_ge(a2_sst[1], 16 * 8 if MP >= 2 else 0)
            sync.nop().then_inc(phb[1], 1)
            for fi in range(max(0, MP - 2)):
                sync.wait_ge(phb[1 + fi], PHB)
                for c in range(NC1):
                    if c >= 1:
                        sync.wait_ge(f_xc[fi], c)
                    sync.dma_start(
                        out=ar(F_X, 2048).rearrange("p (s d) -> p s d", s=2),
                        in_=x_in[c * C1:(c + 1) * C1, :]
                        .rearrange("(s p) d -> p s d", p=P)
                    ).then_inc(f_ldx[fi], 16)
                    if fi == 1:
                        if c == 0:
                            sync.wait_ge(f_deq[1], 24)
                        if c >= 1:
                            sync.wait_ge(f_st[1], 32 * c)
                        sync.dma_start(
                            out=ar(F_S, 2048).rearrange("p (s d) -> p s d", s=2),
                            in_=s_sc[2 * c:2 * c + 2].rearrange("s p d -> p s d")
                        ).then_inc(f_ldz, 16)
                        sync.dma_start(
                            out=ar(F_ZP, 2048).rearrange("p (s d) -> p s d", s=2),
                            in_=z2p_sc[c].rearrange("s p d -> p s d")
                        ).then_inc(f_ldz, 16)
                    if fi == 0:
                        sync.wait_ge(f_zev[0], 2 * c + 2)
                        inst = sync.dma_start(
                            out=z2p_sc[c].rearrange("s p d -> p s d"),
                            in_=ar(F_ZP, 2048).rearrange("p (s d) -> p s d", s=2))
                        inst.then_inc(f_st[0], 16)
                if fi == 0:
                    sync.wait_ge(f_st[0], 16 * NC1)
                    sync.nop().then_inc(phb[2], 1)
            sync.wait_ge(f_st[1], 32 * NC1 if MP >= 4 else 0)

        @block.gpsimd
        def _(gpsimd):
            gpsimd.dma_start(out=grep[:], in_=lng.partition_broadcast(P)
                             ).then_inc(ldcg, 16)
            gpsimd.dma_start(out=brep[:], in_=lnb.partition_broadcast(P)
                             ).then_inc(ldcg, 16)
            gpsimd.dma_start(out=b2rep[:], in_=b2.partition_broadcast(P)
                             ).then_inc(ldcg, 16)
            gpsimd.dma_start(out=out, in_=x_in).then_inc(x2o, 16)
            gpsimd.engine_nop().then_inc(phb[0], 1)
            gpsimd.wait_ge(phb[0], PHB)
            gpsimd.engine_nop().then_inc(phb[1], 1)
            gpsimd.wait_ge(phb[1], PHB if MP >= 2 else 1)
            for fi in range(max(0, MP - 2)):
                if fi == 1:
                    gpsimd.engine_nop().then_inc(phb[2], 1)
                    gpsimd.wait_ge(phb[2], PHB)
                for kt in range(8):
                    if kt >= 2:
                        gpsimd.wait_ge(f_deq[fi], kt - 1)
                    gpsimd.dma_start(
                        out=ar(F_DEQ[kt % 2], 2048),
                        in_=fc1q[kt * 128:(kt + 1) * 128, fi * 2048:(fi + 1) * 2048]
                    ).then_inc(f_dld[fi][kt % 2], 16)

                for i in range(16):
                    kt = fi * 16 + i
                    gpsimd.wait_ge(f_deq[fi], 7 + i)
                    gpsimd.dma_start(
                        out=ar(F_IC2[i % 2], 1024),
                        in_=fc2q[kt * 128:(kt + 1) * 128, :]
                    ).then_inc(f_dld[fi][i % 2], 16)

                if fi == 1:
                    gpsimd.wait_ge(x2o, 16)
                    for c in range(NC1):
                        gpsimd.wait_ge(f_ldz, 32 * (c + 1))
                        gpsimd.wait_ge(f_lnr[1], c + 1)
                        og2 = ar(F_ZP, 2048)
                        og3 = og2.rearrange("r (s d) -> r s d", d=1024)
                        gpsimd.tensor_add(og2, og2, ar(F_XN, 2048))
                        gpsimd.drain()
                        gpsimd.tensor_tensor(og3, og3,
                                             b2rep[:].unsqueeze(1).broadcast_to([P, 2, D]),
                                             ALU.add)
                        gpsimd.drain()
                        gpsimd.engine_nop().then_inc(f_og0, 1)
                        for s_ in range(2):
                            gpsimd.wait_ge(f_p5, 2 * c + s_ + 1)
                            og = ar(F_ZP + s_ * 1024, 1024)
                            gpsimd.tensor_add(og, og, ar(F_DEQ[0] + s_ * 1024, 1024))
                            gpsimd.drain()
                            gpsimd.engine_nop().then_inc(f_gv, 1)
                            gpsimd.wait_ge(f_gs, 2 * c + s_ + 1)
                            gpsimd.tensor_mul(og, og, ar(F_S + s_ * 1024, 1024))
                            gpsimd.drain()
                            gpsimd.dma_start(
                                out=out[c * C1 + s_ * P:c * C1 + (s_ + 1) * P, :],
                                in_=og, accum_op=ALU.add
                            ).then_inc(f_st[1], 16)

        @block.vector
        def _(vector):
            vd = DrainAfter(vector)
            vector.wait_ge(ldc, 48)
            vector.wait_ge(ldcg, 48)
            for g24 in range(24):
                wofsh, wofsl = [(WQTH, WQTL), (WKTH, WKTL), (WVTH, WVTL)][g24 // 8]
                et = g24 % 8
                for grp in range(2):
                    gi = 2 * g24 + grp
                    vector.wait_ge(a1_tr, gi + 1)
                    sel = lambda base: WB[:, base:base + 8192].rearrange(
                        "r (dt e) -> r dt e", dt=8)[
                        :, 4 * grp:4 * grp + 4, et * 128:et * 128 + 128]
                    psrc = bk(gi % 2).rearrange("r (j t) -> r j t", j=4)
                    vd.tensor_copy(sel(wofsh), psrc)
                    vector.tensor_tensor(sel(wofsl), psrc, sel(wofsh),
                                         ALU.subtract).then_inc(a1_wev, 1)
            for c in range(NC1):
                vector.wait_ge(a1_ldx[c % 2], 16 * (c // 2 + 1))
                xb = A1_XB[c % 2]
                x2d = ar(xb, 2048)
                x3d = x2d.rearrange("r (s d) -> r s d", d=1024)
                pers = stats[:, 8 * c:8 * c + 8].rearrange("r (s k) -> r s k", k=4)
                mcols = pers[:, :, 0:1]
                rcols = pers[:, :, 1:2]
                m2cols = pers[:, :, 2:3]
                r2cols = pers[:, :, 3:4]
                acols = stats[:, 64:66].unsqueeze(2)
                iacols = stats[:, 66:68].unsqueeze(2)
                ycols = stats[:, 68:70].unsqueeze(2)
                tcols = stats[:, 70:72].unsqueeze(2)
                tmp2d = ar(A1_TMP, 2048)
                tmp3d = tmp2d.rearrange("r (s d) -> r s d", d=1024)

                def stats_block(srcap2, srcap3, mo, ao):
                    vd.reduce_sum(mo, srcap3, axis=AX)
                    vd.tensor_scalar_mul(mo, mo, 1.0 / D)
                    vd.tensor_mul(tmp2d, srcap2, srcap2)
                    vd.reduce_sum(ao, tmp3d, axis=AX)
                    vd.tensor_scalar_mul(ao, ao, 1.0 / D)
                    vd.tensor_mul(tcols, mo, mo)
                    vd.tensor_sub(ao, ao, tcols)
                    vd.tensor_scalar_add(ao, ao, EPS_LN)
                    vd.reciprocal(iacols, ao)

                def polish(ro):
                    vd.tensor_mul(tcols, ycols, ycols)
                    vd.tensor_mul(tcols, tcols, acols)
                    vd.tensor_scalar(tcols, tcols, -0.5, 1.5, ALU.mult, ALU.add)
                    vd.tensor_mul(ro, tcols, ycols)

                stats_block(x2d, x3d, mcols, acols)
                vector.engine_nop().then_inc(a1_vr, 1)
                vector.wait_ge(a1_sq, 2 * c + 1)
                if c >= 1:
                    vector.wait_ge(a1_tr, 48 + 4 * c)
                polish(rcols)
                for s_ in range(2):
                    sc_ = stats[:, (8 * c + 4 * s_):(8 * c + 4 * s_) + 4]
                    vd.tensor_scalar(ar(A1_XN + s_ * 1024, 1024),
                                     ar(xb + s_ * 1024, 1024),
                                     sc_[:, 0:1], sc_[:, 1:2],
                                     ALU.subtract, ALU.mult)
                inst = vector.engine_nop()
                inst.then_inc(a1_xc, 1)
                vector.engine_nop().then_inc(a1_xnr, 1)
                xn2d = ar(A1_XN, 2048)
                xn3d = xn2d.rearrange("r (s d) -> r s d", d=1024)
                stats_block(xn2d, xn3d, m2cols, acols)
                vector.engine_nop().then_inc(a1_vr, 1)
                vector.wait_ge(a1_sq, 2 * c + 2)
                polish(r2cols)
                for g in range(4):
                    vector.wait_ge(a1_tr, 48 + 4 * c + g + 1)
                    if c >= 1 and g == 0:
                        vector.wait_ge(a1_vmm, 4 * c)
                    half, s_ = divmod(g, 2)
                    sel = lambda base: ARB[:, base:base + 2048].rearrange(
                        "r (dt t) -> r dt t", dt=8)[
                        :, 4 * half:4 * half + 4, s_ * 128:s_ * 128 + 128]
                    psrc = bk(g % 2).rearrange("r (j t) -> r j t", j=4)
                    vd.tensor_copy(sel(A1_XNTH), psrc)
                    vector.tensor_tensor(sel(A1_XNTL), psrc, sel(A1_XNTH),
                                         ALU.subtract).then_inc(a1_xev, 1)
            vector.engine_nop().then_inc(phb[0], 1)
            vector.wait_ge(phb[0], PHB)
            if MP >= 2:
                vd.memset(kvsb[0][:], 0.0)
            for sl in range(2 if MP >= 2 else 0):
                for sub in range(2):
                    vd.memset(WA[:, A2_VA[sl] + sub * 1040:A2_VA[sl] + sub * 1040 + 1040]
                              .rearrange("r (h e) -> r h e", e=65)[:, :, 64:65], 1.0)
            for et8 in range(8 if MP >= 2 else 0):
                for grp in range(2):
                    gi = 2 * et8 + grp
                    vector.wait_ge(a2_tr, gi + 1)
                    sel = lambda base: WB[:, base:base + 8192].rearrange(
                        "r (dt e) -> r dt e", dt=8)[
                        :, 4 * grp:4 * grp + 4, et8 * 128:et8 * 128 + 128]
                    psrc = bk(gi % 2).rearrange("r (j t) -> r j t", j=4)
                    vd.tensor_copy(sel(WOTH), psrc)
                    vector.tensor_tensor(sel(WOTL), psrc, sel(WOTH),
                                         ALU.subtract).then_inc(a2_wev, 1)
            for c in range(8 if MP >= 2 else 0):
                sl = c % 2
                for g in range(4):
                    vector.wait_ge(a2_tr, 16 + 4 * c + g + 1)
                    vector.tensor_copy(
                        WA[:, A2_KFT + g * 512:A2_KFT + g * 512 + 512],
                        bk(4 + g % 2)).then_inc(a2_ktre, 1)
                for h in range(16):
                    vector.wait_ge(a2_amm, 16 * c + h + 1)
                    vector.tensor_tensor(
                        ar(A2_A + h * 384, 384),
                        bk(2 + h % 2, 0, 384),
                        maskw_s[:],
                        ALU.mult).then_inc(a2_aev, 1)
                vector.wait_ge(a2_dkv, c + 1)
                ksrc, kdst = kvsb[c % 2], kvsb[(c + 1) % 2]
                vector.tensor_add(kdst[:, 0:455], ksrc[:, 0:455], PS[0:64, 0:455])
                vector.tensor_add(kdst[:, 455:910], ksrc[:, 455:910], PS[0:64, 512:967])
                vd.tensor_add(kdst[:, 910:1040], ksrc[:, 910:1040], PS[0:64, 3584:3714])
                vector.engine_nop().then_inc(a2_kvu, 1)
                for ts_ in range(2):
                    vector.wait_ge(a2_nmm, 2 * c + ts_ + 1)
                    den = AR[:, A2_DEN:A2_DEN + 16]
                    rden = AR[:, A2_DEN + 16:A2_DEN + 32]
                    for bi, (h0, nh) in enumerate([(0, 7), (7, 7), (14, 2)]):
                        vector.tensor_copy(
                            den[:, h0:h0 + nh].unsqueeze(2),
                            ps_strided(NUM_PS + 512 * bi + 64, 65, nh, 1))
                    vector.drain()
                    vd.tensor_scalar_add(den[:], den[:], EPS_DEN)
                    vd.reciprocal(rden[:], den[:])
                    for bi, (h0, nh) in enumerate([(0, 7), (7, 7), (14, 2)]):
                        vector.tensor_tensor(
                            ar(A2_NUMR, 1024).rearrange("r (h e) -> r h e", e=64)
                            [:, h0:h0 + nh, :],
                            ps_strided(NUM_PS + 512 * bi, 65, nh, 64),
                            rden[:, h0:h0 + nh].unsqueeze(2).broadcast_to([P, nh, 64]),
                            ALU.mult)
                    vector.engine_nop().then_inc(a2_nev, 1)
                    for g in range(2):
                        vector.wait_ge(a2_ntr, 4 * c + 2 * ts_ + g + 1)
                        vd.tensor_copy(arb(2 * A2_NUMRT + g * 512, 512), bk(2 + g))
                        vector.tensor_tensor(arb(2 * A2_NUMRT + 1024 + g * 512, 512),
                                             bk(2 + g),
                                             arb(2 * A2_NUMRT + g * 512, 512),
                                             ALU.subtract).then_inc(a2_ntre, 1)
            vector.engine_nop().then_inc(phb[1], 1)
            for fi in range(max(0, MP - 2)):
                vector.wait_ge(phb[1 + fi], PHB)
                if fi == 0:
                    vd.tensor_scalar_mul(b2rep[:], b2rep[:], 0.5)
                for ti in range(8):
                    is1 = ti < 8
                    n = 2048 if is1 else 1024
                    slot = (ti % 2) if is1 else ((ti - 8) % 2)
                    vector.wait_ge(f_dld[fi][ti % 2], 16 * (ti // 2 + 1))
                    vector.wait_ge(f_dls[fi][ti % 2], 16 * (ti // 2 + 1))
                    if is1:
                        dh = wb(W1AH + ti * 2048, n)
                        dl = wb(W1AL + ti * 2048, n)
                    else:
                        dh = wb(W2AH + (ti - 8) * 1024, n)
                        dl = wb(W2AL + (ti - 8) * 1024, n)
                    tmp = ar(F_XN, n)
                    vd.scalar_tensor_tensor(
                        tmp, ar(F_DEQ[slot], n), -128.0, ar(F_SREP[slot], n),
                        ALU.add, ALU.mult)
                    vd.tensor_copy(dh, tmp)
                    vd.tensor_sub(dl, tmp, dh).then_inc(f_deq[fi], 1)
                for c in range(NC1):
                    vector.wait_ge(f_ldx[fi], 16 * (c + 1))
                    if c >= 1:
                        vector.wait_ge(f_tr[fi], 4 * c)
                        if fi == 0:
                            vector.wait_ge(f_st[0], 16 * c)
                        else:
                            vector.wait_ge(f_og0, c)
                    for s_ in range(2):
                        sc_ = stats[:, (2 * c + s_) * 4:(2 * c + s_) * 4 + 4]
                        xap = ar(F_X + s_ * 1024, 1024)
                        xnap = ar(F_XN + s_ * 1024, 1024)
                        vd.tensor_scalar(xnap, xap, sc_[:, 0:1], sc_[:, 1:2],
                                         ALU.subtract, ALU.mult)
                        vd.tensor_scalar(ar(F_LN2 + s_ * 1024, 1024), xnap,
                                         sc_[:, 2:3], sc_[:, 3:4],
                                         ALU.subtract, ALU.mult)
                    gb = ar(F_LN2, 2048).rearrange("r (s d) -> r s d", d=1024)
                    vd.tensor_tensor(gb, gb,
                                     grep[:].unsqueeze(1).broadcast_to([P, 2, D]),
                                     ALU.mult)
                    vd.tensor_tensor(gb, gb,
                                     brep[:].unsqueeze(1).broadcast_to([P, 2, D]),
                                     ALU.add)
                    vector.engine_nop().then_inc(f_lnr[fi], 1)
                    vector.engine_nop().then_inc(f_xc[fi], 1)
                    for g in range(4):
                        vector.wait_ge(f_tr[fi], 4 * c + g + 1)
                        half, s_ = divmod(g, 2)
                        sel = lambda base: ARB[:, base:base + 2048].rearrange(
                            "r (dt t) -> r dt t", dt=8)[
                            :, 4 * half:4 * half + 4, s_ * 128:s_ * 128 + 128]
                        psrc = bk(g % 2).rearrange("r (j t) -> r j t", j=4)
                        vd.tensor_copy(sel(F_LN2TH), psrc)
                        vector.tensor_tensor(sel(F_LN2TL), psrc, sel(F_LN2TH),
                                             ALU.subtract).then_inc(f_tre[fi], 1)
                    if c == 0:
                        for ti in range(8, 24):
                            n = 1024
                            slot = (ti - 8) % 2
                            vector.wait_ge(f_dld[fi][ti % 2], 16 * (ti // 2 + 1))
                            vector.wait_ge(f_dls[fi][ti % 2], 16 * (ti // 2 + 1))
                            dh = wb(W2AH + (ti - 8) * 1024, n)
                            dl = wb(W2AL + (ti - 8) * 1024, n)
                            stg = ar(F_IC2[slot], n)
                            vd.scalar_tensor_tensor(
                                stg, stg, -128.0, ar(F_SR2[slot], n),
                                ALU.add, ALU.mult)
                            vd.tensor_copy(dh, stg)
                            vd.tensor_sub(dl, stg, dh).then_inc(f_deq[fi], 1)
                    for ft in range(16):
                        vector.wait_ge(f_ysh[fi], 16 * c + ft + 1)
                        vector.tensor_sub(
                            arb(F_YL + ft * 256, 256),
                            YTMP[:, (ft % 2) * 256:(ft % 2) * 256 + 256],
                            arb(F_YH + ft * 256, 256)).then_inc(f_yev[fi], 1)

                if fi == 0:
                    vector.engine_nop().then_inc(phb[2], 1)

        @block.scalar
        def _(scalar):
            scalar.wait_ge(ldc, 48)
            scalar.wait_ge(ldcg, 48)
            for widx, wsrc in enumerate([wq, wk, wv]):
                for et in range(8):
                    g = widx * 8 + et
                    if g >= 2:
                        scalar.wait_ge(a1_tr, 2 * g - 2)
                    scalar.dma_start(out=WA[:, A1_WST[g % 2]:A1_WST[g % 2] + 1024],
                                     in_=wsrc[et * 128:(et + 1) * 128, :]
                                     ).then_inc(a1_wld[g % 2], 16)
            for c in range(NC1):
                for rnd in range(2):
                    scalar.wait_ge(a1_vr, 2 * c + rnd + 1)
                    scalar.activation(stats[:, 68:70], stats[:, 66:68],
                                      AF.Sqrt).then_inc(a1_sq, 1)
                for nt in range(8):
                    qi = 8 * c + nt
                    scalar.wait_ge(a1_qmq, qi + 1)
                    if nt == 0 and c >= 1:
                        scalar.wait_ge(st_q, 16 * c)
                    scalar.activation(ar(A1_QF, 2048)[:, nt * 256:nt * 256 + 256],
                                      bk(2 + nt % 2, 0, 256), AF.Exp
                                      ).then_inc(a1_qeq, 1)
                scalar.drain()
                scalar.activation(ar(A1_QF, 2048), ar(A1_QF, 2048), AF.Ln, bias=1.0)
                scalar.drain()
                scalar.activation(ar(A1_QF, 2048), ar(A1_QF, 2048), AF.Copy,
                                  bias=EPS_PHI).then_inc(a1_qr, 1)
                for nt in range(8):
                    qi = 8 * c + nt
                    scalar.wait_ge(a1_qmk, qi + 1)
                    if nt == 0 and c >= 1:
                        scalar.wait_ge(st_k, 16 * c)
                    scalar.activation(ar(A1_KF, 2048)[:, nt * 256:nt * 256 + 256],
                                      bk(4 + nt % 2, 0, 256), AF.Exp
                                      ).then_inc(a1_qek, 1)
                scalar.drain()
                scalar.activation(ar(A1_KF, 2048), ar(A1_KF, 2048), AF.Ln, bias=1.0)
                scalar.drain()
                scalar.activation(ar(A1_KF, 2048), ar(A1_KF, 2048), AF.Copy,
                                  bias=EPS_PHI).then_inc(a1_qr, 1)
                for vi4 in range(4):
                    s_, jt = divmod(vi4, 2)
                    vi = 4 * c + vi4
                    scalar.wait_ge(a1_vmm, vi + 1)
                    if vi4 == 0 and c >= 1:
                        scalar.wait_ge(st_v, 16 * c)
                    scalar.copy(
                        ar(A1_V, 2048)[:, s_ * 1024 + jt * 512:s_ * 1024 + jt * 512 + 512],
                        bk(6 + jt, 0, 512)).then_inc(a1_ve, 1)
                    if c == NC1 - 1 and vi4 == 3:
                        scalar.nop().then_inc(phb[0], 1)
            scalar.wait_ge(phb[0], PHB)
            for et in range(8 if MP >= 2 else 0):
                if et >= 1:
                    scalar.wait_ge(a2_tr, 2 * et)
                scalar.dma_start(out=WA[:, A2_WST[et % 2]:A2_WST[et % 2] + 1024],
                                 in_=wo[et * 128:(et + 1) * 128, :]
                                 ).then_inc(a2_wld[et % 2], 16)
            for c in range(8 if MP >= 2 else 0):
                sl = c % 2
                for ts_ in range(2):
                    scalar.wait_ge(a2_smm, 2 * c + ts_ + 1)
                    if c >= 2 and ts_ == 0:
                        scalar.wait_ge(a2_sst[sl], 32 * (c // 2))
                    scalar.copy(ar(A2_S[sl] + ts_ * 1024, 512), bk(4, 0, 512))
                    scalar.copy(ar(A2_S[sl] + ts_ * 1024 + 512, 512), bk(5, 0, 512)
                                ).then_inc(a2_sev, 1)
                if c == 7:
                    scalar.nop().then_inc(phb[1], 1)
            for fi in range(max(0, MP - 2)):
                scalar.wait_ge(phb[1 + fi], PHB)
                for ti in range(24):
                    is1 = ti < 8
                    if ti >= 2:
                        scalar.wait_ge(f_deq[fi], ti - 1)
                    if is1:
                        scalar.dma_start(
                            out=ar(F_SREP[ti % 2], 2048),
                            in_=s1[ti:ti + 1, fi * 2048:(fi + 1) * 2048]
                            .partition_broadcast(P)
                        ).then_inc(f_dls[fi][ti % 2], 16)
                    else:
                        kt = fi * 16 + (ti - 8)
                        if ti == 8:
                            scalar.wait_ge(f_tr[fi], 4)
                        scalar.dma_start(
                            out=ar(F_SR2[ti % 2], 1024),
                            in_=s2[kt:kt + 1, :].partition_broadcast(P)
                        ).then_inc(f_dls[fi][ti % 2], 16)
                for c in range(NC1):
                    for ft in range(16):
                        yi = 16 * c + ft
                        scalar.wait_ge(f_ymm[fi], yi + 1)
                        if ft == 0 and c >= 1:
                            scalar.wait_ge(f_zmm[fi], 2 * c)
                        if yi >= 2:
                            scalar.wait_ge(f_yev[fi], yi - 1)
                        scalar.activation(
                            YTMP[:, (ft % 2) * 256:(ft % 2) * 256 + 256],
                            bk(2 + ft % 2, 0, 256), AF.Silu,
                            bias=b1_s[:, fi * 16 + ft:fi * 16 + ft + 1])
                        scalar.drain()
                        scalar.copy(arb(F_YH + ft * 256, 256),
                                    YTMP[:, (ft % 2) * 256:(ft % 2) * 256 + 256]
                                    ).then_inc(f_ysh[fi], 1)
                        if fi == 0 and c == NC1 - 1 and ft == 15:
                            scalar.nop().then_inc(phb[2], 1)
                    if fi == 0:
                        for s_ in range(2):
                            scalar.wait_ge(f_zmm[0], 2 * c + s_ + 1)
                            scalar.activation(
                                ar(F_ZP + s_ * 1024, 1024),
                                PS[:, (4 + 2 * s_) * 512:(4 + 2 * s_) * 512 + 1024],
                                AF.Copy, scale=0.5).then_inc(f_zev[0], 1)
                    else:
                        for s_ in range(2):
                            scalar.wait_ge(f_zmm[1], 2 * c + s_ + 1)
                            if c >= 1 and s_ == 0:
                                scalar.wait_ge(f_gv, 2 * c)
                            scalar.activation(
                                ar(F_DEQ[0] + s_ * 1024, 1024),
                                PS[:, (4 + 2 * s_) * 512:(4 + 2 * s_) * 512 + 1024],
                                AF.Copy, scale=0.5).then_inc(f_p5, 1)
                        for s_ in range(2):
                            scalar.wait_ge(f_gv, 2 * c + s_ + 1)
                            scalar.activation(ar(F_ZP + s_ * 1024, 1024),
                                              ar(F_ZP + s_ * 1024, 1024), AF.Silu
                                              ).then_inc(f_gs, 1)

        @block.tensor
        def _(tensor):
            tensor.wait_ge(ldc, 48)
            tensor.wait_ge(ldcg, 48)
            for g in range(24):
                tensor.wait_ge(a1_wld[g % 2], 16 * (g // 2 + 1))
                for grp in range(2):
                    gi = 2 * g + grp
                    if gi >= 2:
                        tensor.wait_ge(a1_wev, gi - 1)
                    for j in range(4):
                        dt = 4 * grp + j
                        inst = tensor.transpose(
                            bk(gi % 2, j * 128, 128),
                            WA[:, A1_WST[g % 2] + dt * 128:A1_WST[g % 2] + dt * 128 + 128],
                            ident_s[:])
                        if j == 3:
                            inst.then_inc(a1_tr, 1)
            for c in range(NC1):
                tensor.wait_ge(a1_xnr, c + 1)
                for g in range(4):
                    li = 4 * c + g
                    if li >= 2:
                        tensor.wait_ge(a1_xev, li - 1)
                    elif li < 2:
                        tensor.wait_ge(a1_wev, 47 + (g % 2))
                    half, s_ = divmod(g, 2)
                    for j in range(4):
                        dt = 4 * half + j
                        inst = tensor.transpose(
                            bk(g % 2, j * 128, 128),
                            ar(A1_XN + s_ * 1024, 1024)[:, dt * 128:dt * 128 + 128],
                            ident_s[:])
                        if j == 3:
                            inst.then_inc(a1_tr, 1)
                if c == 0:
                    tensor.wait_ge(a1_wev, 48)
                tensor.wait_ge(a1_xev, 4 * c + 4)
                for nt in range(8):
                    qi = 8 * c + nt
                    if qi >= 2:
                        tensor.wait_ge(a1_qeq, qi - 1)
                    for dt in range(8):
                        inst = mm3(tensor, bk(2 + nt % 2, 0, 256),
                                   wb(WQTH + dt * 1024 + nt * 128, 128),
                                   wb(WQTL + dt * 1024 + nt * 128, 128),
                                   arb(A1_XNTH + dt * 256, 256),
                                   arb(A1_XNTL + dt * 256, 256),
                                   start=(dt == 0), stop=(dt == 7))
                        if dt == 7:
                            inst.then_inc(a1_qmq, 1)
                for nt in range(8):
                    qi = 8 * c + nt
                    if qi >= 2:
                        tensor.wait_ge(a1_qek, qi - 1)
                    for dt in range(8):
                        inst = mm3(tensor, bk(4 + nt % 2, 0, 256),
                                   wb(WKTH + dt * 1024 + nt * 128, 128),
                                   wb(WKTL + dt * 1024 + nt * 128, 128),
                                   arb(A1_XNTH + dt * 256, 256),
                                   arb(A1_XNTL + dt * 256, 256),
                                   start=(dt == 0), stop=(dt == 7))
                        if dt == 7:
                            inst.then_inc(a1_qmk, 1)
                for vi4 in range(4):
                    s_, jt = divmod(vi4, 2)
                    vi = 4 * c + vi4
                    if vi >= 2:
                        tensor.wait_ge(a1_ve, vi - 1)
                    for dt in range(8):
                        inst = mm3(tensor, bk(6 + jt, 0, 512),
                                   arb(A1_XNTH + dt * 256 + s_ * 128, 128),
                                   arb(A1_XNTL + dt * 256 + s_ * 128, 128),
                                   wb(WVTH + dt * 1024 + jt * 512, 512),
                                   wb(WVTL + dt * 1024 + jt * 512, 512),
                                   start=(dt == 0), stop=(dt == 7))
                        if dt == 7:
                            inst.then_inc(a1_vmm, 1)
                if c == NC1 - 1 and vi4 == 3:
                    tensor.nop().then_inc(phb[0], 1)
            tensor.wait_ge(phb[0], PHB)
            for et in range(8 if MP >= 2 else 0):
                tensor.wait_ge(a2_wld[et % 2], 16 * (et // 2 + 1))
                for grp in range(2):
                    gi = 2 * et + grp
                    if gi >= 2:
                        tensor.wait_ge(a2_wev, gi - 1)
                    for j in range(4):
                        dt = 4 * grp + j
                        inst = tensor.transpose(
                            bk(gi % 2, j * 128, 128),
                            WA[:, A2_WST[et % 2] + dt * 128:A2_WST[et % 2] + dt * 128 + 128],
                            ident_s[:])
                        if j == 3:
                            inst.then_inc(a2_tr, 1)
            for c in range(8 if MP >= 2 else 0):
                sl = c % 2
                tensor.wait_ge(a2_ld[sl], 96 * (c // 2 + 1))
                if c >= 1:
                    tensor.wait_ge(a2_sev, 2 * c)
                for g in range(4):
                    if 4 * c + g >= 2:
                        tensor.wait_ge(a2_ktre, 4 * c + g - 1)
                    for j in range(8):
                        f = 8 * g + j
                        h, sub = f // 2, f % 2
                        inst = tensor.transpose(
                            bk(4 + g % 2, j * 64, 64),
                            WA[0:64, A2_KF[sl] + h * 256 + sub * 128:
                               A2_KF[sl] + h * 256 + sub * 128 + 128],
                            ident_s[0:64, 0:64])
                        if j == 7:
                            inst.then_inc(a2_tr, 1)
                for h in range(16):
                    ai = 16 * c + h
                    if ai >= 2:
                        tensor.wait_ge(a2_aev, ai - 1)
                    if c >= 1 and h < 2:
                        tensor.wait_ge(a2_ntre, 4 * c)
                    qcol = A2_QF[sl] + h * 256
                    kcol = A2_KF[sl] + h * 256
                    tensor.matmul(
                        bk(2 + h % 2, 0, 256),
                        WA[0:64, kcol:kcol + 128],
                        WA[0:64, qcol:qcol + 256],
                        start=True, stop=True)
                    inst = tensor.matmul(
                        bk(2 + h % 2, 256, 128),
                        WA[0:64, kcol + 128:kcol + 256],
                        WA[0:64, qcol + 128:qcol + 256],
                        start=True, stop=True)
                    inst.then_inc(a2_amm, 1)
                tensor.wait_ge(a2_ktre, 4 * c + 4)
                if c >= 1:
                    tensor.wait_ge(a2_kvu, c)
                if c == 0:
                    tensor.wait_ge(a2_wev, 16)
                for h in range(16):
                    col = dkv_col(h)
                    for sub in range(2):
                        inst = tensor.matmul(
                            PS[0:64, col:col + 65],
                            WA[:, A2_KFT + h * 128 + sub * 64:
                               A2_KFT + h * 128 + sub * 64 + 64],
                            WA[:, A2_VA[sl] + sub * 1040 + h * 65:
                               A2_VA[sl] + sub * 1040 + h * 65 + 65],
                            start=(sub == 0), stop=(sub == 1))
                        if h == 15 and sub == 1:
                            inst.then_inc(a2_dkv, 1)
                for ts_ in range(2):
                    if ts_ == 0:
                        tensor.wait_ge(a2_ktre, 4 * c + 4)
                        tensor.wait_ge(a2_aev, 16 * c + 16)
                    else:
                        tensor.wait_ge(a2_sev, 2 * c + 1)
                    last = None
                    for h in range(16):
                        reg = PS[:, NUM_PS + num_off(h):NUM_PS + num_off(h) + 65]
                        va_h = lambda sub: WA[:, A2_VA[sl] + sub * 1040 + h * 65:
                                              A2_VA[sl] + sub * 1040 + h * 65 + 65]
                        if ts_ == 0:
                            last = tensor.matmul(reg, ar(A2_A + h * 384, 128), va_h(0),
                                                 start=True, stop=(c == 0))
                        else:
                            tensor.matmul(reg, ar(A2_A + h * 384 + 128, 128), va_h(0),
                                          start=True, stop=False)
                            last = tensor.matmul(reg, ar(A2_A + h * 384 + 256, 128),
                                                 va_h(1), start=False, stop=(c == 0))
                        if c > 0:
                            qcol = A2_QF[sl] + h * 256 + ts_ * 128
                            last = tensor.matmul(reg, WA[0:64, qcol:qcol + 128],
                                                 kvsb[c % 2][:, h * 65:h * 65 + 65],
                                                 start=False, stop=True)
                    last.then_inc(a2_nmm, 1)
                    tensor.wait_ge(a2_nev, 2 * c + ts_ + 1)
                    for g in range(2):
                        for j in range(4):
                            dt = 4 * g + j
                            inst = tensor.transpose(
                                bk(2 + g, j * 128, 128),
                                ar(A2_NUMR, 1024)[:, dt * 128:dt * 128 + 128],
                                ident_s[:])
                            if j == 3:
                                inst.then_inc(a2_ntr, 1)
                    tensor.wait_ge(a2_ntre, 4 * c + 2 * ts_ + 2)
                    for jt in range(2):
                        for nt in range(8):
                            inst = mm3(tensor, bk(4 + jt, 0, 512),
                                       arb(2 * A2_NUMRT + nt * 128, 128),
                                       arb(2 * A2_NUMRT + 1024 + nt * 128, 128),
                                       wb(WOTH + nt * 1024 + jt * 512, 512),
                                       wb(WOTL + nt * 1024 + jt * 512, 512),
                                       start=(nt == 0), stop=(nt == 7))
                            if jt == 1 and nt == 7:
                                inst.then_inc(a2_smm, 1)
                tensor.nop().then_inc(a2_cons, 1)
                if c == 7:
                    tensor.nop().then_inc(phb[1], 1)
            for fi in range(max(0, MP - 2)):
                tensor.wait_ge(phb[1 + fi], PHB)
                for c in range(NC1):
                    tensor.wait_ge(f_lnr[fi], c + 1)
                    if c >= 1:
                        if fi == 1:
                            tensor.wait_ge(f_p5, 2 * c)
                        else:
                            tensor.wait_ge(f_zev[0], 2 * c)
                    for g in range(4):
                        gi = 4 * c + g
                        if gi >= 2:
                            tensor.wait_ge(f_tre[fi], gi - 1)
                        half, s_ = divmod(g, 2)
                        for j in range(4):
                            dt = 4 * half + j
                            inst = tensor.transpose(
                                bk(g % 2, j * 128, 128),
                                ar(F_LN2 + s_ * 1024, 1024)[:, dt * 128:dt * 128 + 128],
                                ident_s[:])
                            if j == 3:
                                inst.then_inc(f_tr[fi], 1)
                    tensor.wait_ge(f_tre[fi], 4 * c + 4)
                    for ft in range(17):
                        if ft < 16:
                            yi = 16 * c + ft
                            if yi >= 2:
                                tensor.wait_ge(f_ysh[fi], yi - 1)
                            for kt in range(8):
                                if c == 0 and ft == 0:
                                    tensor.wait_ge(f_deq[fi], kt + 1)
                                inst = mm3(tensor, bk(2 + ft % 2, 0, 256),
                                           wb(W1AH + kt * 2048 + ft * 128, 128),
                                           wb(W1AL + kt * 2048 + ft * 128, 128),
                                           arb(F_LN2TH + kt * 256, 256),
                                           arb(F_LN2TL + kt * 256, 256),
                                           start=(kt == 0), stop=(kt == 7))
                                if kt == 7:
                                    inst.then_inc(f_ymm[fi], 1)
                        if ft >= 1:
                            zt = ft - 1
                            if c == 0:
                                tensor.wait_ge(f_deq[fi], 9 + zt)
                            tensor.wait_ge(f_yev[fi], 16 * c + zt + 1)
                            for s_ in range(2):
                                for jt in range(2):
                                    inst = mm3(tensor, bk(4 + 2 * s_ + jt, 0, 512),
                                               arb(F_YH + zt * 256 + s_ * 128, 128),
                                               arb(F_YL + zt * 256 + s_ * 128, 128),
                                               wb(W2AH + zt * 1024 + jt * 512, 512),
                                               wb(W2AL + zt * 1024 + jt * 512, 512),
                                               start=(zt == 0), stop=(zt == 15))
                                    if zt == 15 and jt == 1:
                                        inst.then_inc(f_zmm[fi], 1)
                if fi == 0 and c == NC1 - 1:
                    tensor.nop().then_inc(phb[2], 1)

    return nc


def _maskw():
    import ml_dtypes
    tril = (np.arange(P)[:, None] <= np.arange(P)[None, :])
    m = np.concatenate([tril, np.ones((P, P), bool), tril], axis=1)
    return m.astype(ml_dtypes.bfloat16)


_BUILT = {}


def kernel(L0_attn_x, L0_attn_Wq, L0_attn_Wk, L0_attn_Wv, L0_attn_Wo,
           L0_ffn_ln_gamma, L0_ffn_ln_beta, L0_ffn_fc1_q, L0_ffn_fc1_scales,
           L0_ffn_fc1_bias, L0_ffn_fc2_q, L0_ffn_fc2_scales, L0_ffn_fc2_bias):
    x = np.ascontiguousarray(np.asarray(L0_attn_x, dtype=np.float32))
    f32 = lambda a: np.ascontiguousarray(np.asarray(a, dtype=np.float32))
    i32 = lambda a: np.ascontiguousarray(np.asarray(a, dtype=np.int32))
    common = {
        "wq": f32(L0_attn_Wq), "wk": f32(L0_attn_Wk),
        "wv": f32(L0_attn_Wv), "wo": f32(L0_attn_Wo),
        "fc1q": i32(L0_ffn_fc1_q), "fc2q": i32(L0_ffn_fc2_q),
        "s1": f32(L0_ffn_fc1_scales), "s2": f32(L0_ffn_fc2_scales),
        "b1": np.ascontiguousarray(f32(L0_ffn_fc1_bias).reshape(32, P).T),
        "b2": f32(L0_ffn_fc2_bias).reshape(1, D),
        "lng": f32(L0_ffn_ln_gamma).reshape(1, D),
        "lnb": f32(L0_ffn_ln_beta).reshape(1, D),
        "identf": np.eye(P, dtype=np.float32),
        "maskw": _maskw(),
    }
    if "nc" not in _BUILT:
        _BUILT["nc"] = build_nc()
    nc = _BUILT["nc"]
    in_maps = [dict(common, x_in=x[b]) for b in range(x.shape[0])]
    res = run_bass_kernel_spmd(nc, in_maps, core_ids=list(range(len(in_maps))))
    return np.stack([r["out"] for r in res.results], axis=0)


# version 09
# speedup vs baseline: 1.0366x; 1.0366x over previous
import numpy as np
from contextlib import ExitStack

import concourse.bass as bass
import concourse.mybir as mybir
from concourse.bass_utils import run_bass_kernel_spmd

F32 = mybir.dt.float32
BF16 = mybir.dt.bfloat16
I32 = mybir.dt.int32
AF = mybir.ActivationFunctionType
ALU = mybir.AluOpType
AX = mybir.AxisListType.X

P = 128
T = 2048
D = 1024
FF = 4096
C1 = 256
NC1 = T // C1
C2 = 128
NC2 = T // C2
EPS_LN = 1e-5
EPS_PHI = 1e-6
EPS_DEN = 1e-6

WQTH, WQTL = 0, 8192
WKTH, WKTL = 16384, 24576
WVTH, WVTL = 32768, 40960
WOTH, WOTL = 49152, 57344
W1AH, W1AL = 0, 16384
W2AH, W2AL = 32768, 49152

A1_XB = [0, 2048]
A1_XN = 4096
A1_XNTH = 12288
A1_XNTL = 14336
A1_QF = 8192
A1_KF = 10240
A1_V = 12288
A1_TMP = 14336
A1_WST = [24576, 25600]

A2_QF = [1024, 5120]
A2_KF = [9216, 13312]
A2_VA = [17408, 19488]
A2_KFT = 21568
A2_A = 0
A2_NUMR = 6144
A2_NUMRT = 7168
A2_S = [8192, 10240]
A2_DEN = 12288
A2_KVS = [12320, 13632]
A2_WST = [0, 0]

F_X = 0
F_XN = 2048
F_LN2 = 4096
F_LN2TH = 12288
F_LN2TL = 14336
F_YH = 16384
F_YL = 20480
F_S = 12288
F_ZP = 14336
F_OUT = 4096
F_DEQ = [8192, 10240]
F_SREP = [12288, 14336]
F_IC2 = [12288, 13312]
F_SR2 = [4096, 5120]

NUM_PS = 2048


def num_off(h):
    if h < 14:
        return 512 * (h // 7) + 65 * (h % 7)
    return 1024 + 65 * (h - 14)


def dkv_col(h):
    if h < 7:
        return 65 * h
    if h < 14:
        return 512 + 65 * (h - 7)
    return 3584 + 65 * (h - 14)


def build_nc(debug=False, max_phase=4):
    MP = max_phase
    nc = bass.Bass("TRN2", target_bir_lowering=False)

    x_in = nc.dram_tensor("x_in", [T, D], F32, kind="ExternalInput").ap()
    wq = nc.dram_tensor("wq", [D, D], F32, kind="ExternalInput").ap()
    wk = nc.dram_tensor("wk", [D, D], F32, kind="ExternalInput").ap()
    wv = nc.dram_tensor("wv", [D, D], F32, kind="ExternalInput").ap()
    wo = nc.dram_tensor("wo", [D, D], F32, kind="ExternalInput").ap()
    fc1q = nc.dram_tensor("fc1q", [D, FF], I32, kind="ExternalInput").ap()
    fc2q = nc.dram_tensor("fc2q", [FF, D], I32, kind="ExternalInput").ap()
    s1 = nc.dram_tensor("s1", [8, FF], F32, kind="ExternalInput").ap()
    s2 = nc.dram_tensor("s2", [32, D], F32, kind="ExternalInput").ap()
    b1 = nc.dram_tensor("b1", [P, 32], F32, kind="ExternalInput").ap()
    b2 = nc.dram_tensor("b2", [1, D], F32, kind="ExternalInput").ap()
    lng = nc.dram_tensor("lng", [1, D], F32, kind="ExternalInput").ap()
    lnb = nc.dram_tensor("lnb", [1, D], F32, kind="ExternalInput").ap()
    identf = nc.dram_tensor("identf", [P, P], F32, kind="ExternalInput").ap()
    maskw = nc.dram_tensor("maskw", [P, 384], BF16, kind="ExternalInput").ap()
    out = nc.dram_tensor("out", [T, D], F32, kind="ExternalOutput").ap()

    ksc = "ExternalOutput" if debug else "Internal"
    qf_sc = nc.dram_tensor("qf_sc", [NC1, P, 2048], F32, kind=ksc).ap()
    kf_sc = nc.dram_tensor("kf_sc", [NC1, P, 2048], F32, kind=ksc).ap()
    v_sc = nc.dram_tensor("v_sc", [NC1, 2, P, D], F32, kind=ksc).ap()
    s_sc = nc.dram_tensor("s_sc", [NC2, P, D], F32, kind=ksc).ap()
    z2p_sc = nc.dram_tensor("z2p_sc", [NC1, 2, P, D], F32, kind="Internal").ap()

    es = ExitStack()
    sbuf = lambda nm, shape, dt=F32: es.enter_context(nc.sbuf_tensor(nm, shape, dt))
    sem = lambda nm: es.enter_context(nc.semaphore(nm))

    with es:
        WA = sbuf("WA", [P, 32768])
        AR = sbuf("AR", [P, 16384])
        ident_s = sbuf("ident_s", [P, P])
        maskw_s = sbuf("maskw_s", [P, 384], BF16)
        stats = sbuf("stats", [P, 72])
        grep = sbuf("grep", [P, D])
        brep = sbuf("brep", [P, D])
        b2rep = sbuf("b2rep", [P, D])
        b1_s = sbuf("b1_s", [P, 32])
        kvs = None
        PS = es.enter_context(nc.psum_tensor("PS", [P, 4096], F32))
        kvsb = [AR[0:64, A2_KVS[i]:A2_KVS[i] + 1040] for i in range(2)]
        WB = WA[:].bitcast(BF16)
        ARB = AR[:].bitcast(BF16)
        YTMP = sbuf("YTMP", [128, 512])

        def wb(off, n):
            return WB[:, off:off + n]

        def arb(off, n):
            return ARB[:, off:off + n]

        def mm3(tensor, out, lh, ll, rh, rl, start, stop):
            tensor.matmul(out, lh, rh, start=start, stop=False)
            tensor.matmul(out, lh, rl, start=False, stop=False)
            return tensor.matmul(out, ll, rh, start=False, stop=stop)

        def ar(off, n):
            return AR[:, off:off + n]

        def bk(b, off=0, n=512):
            return PS[:, 512 * b + off:512 * b + off + n]

        def ps_strided(base, step, count, width):
            return PS[:, base:base + step * count].rearrange(
                "r (c w) -> r c w", w=step)[:, :, 0:width]

        class DrainAfter:
            def __init__(self, eng):
                self._e = eng

            def __getattr__(self, name):
                f = getattr(self._e, name)
                if name in ("wait_ge", "drain", "nop", "engine_nop", "dma_start"):
                    return f
                def g(*a, **k):
                    r = f(*a, **k)
                    self._e.drain()
                    return r
                return g

        ldc = sem("ldc")
        ldcg = sem("ldcg")
        phb = [sem("phb%d" % i) for i in range(3)]
        PHB = 5
        a1_wld = [sem("a1_wld0"), sem("a1_wld1")]
        a1_tr = sem("a1_tr")
        a1_wev = sem("a1_wev")
        a1_xev = sem("a1_xev")
        a1_ldx = [sem("a1_ldx0"), sem("a1_ldx1")]
        a1_xc = sem("a1_xc")
        a1_vr = sem("a1_vr")
        a1_sq = sem("a1_sq")
        a1_xnr = sem("a1_xnr")
        a1_qmq = sem("a1_qmq")
        a1_qeq = sem("a1_qeq")
        a1_qmk = sem("a1_qmk")
        a1_qek = sem("a1_qek")
        a1_vmm = sem("a1_vmm")
        a1_ve = sem("a1_ve")
        a1_qr = sem("a1_qr")
        st_q = sem("st_q")
        st_k = sem("st_k")
        st_v = sem("st_v")
        a2_wld = [sem("a2_wld0"), sem("a2_wld1")]
        a2_tr = sem("a2_tr")
        a2_wev = sem("a2_wev")
        a2_ktre = sem("a2_ktre")
        a2_ld = [sem("a2_ld0"), sem("a2_ld1")]
        a2_cons = sem("a2_cons")
        a2_amm = sem("a2_amm")
        a2_aev = sem("a2_aev")
        a2_nmm = sem("a2_nmm")
        a2_nev = sem("a2_nev")
        a2_ntr = sem("a2_ntr")
        a2_ntre = sem("a2_ntre")
        a2_smm = sem("a2_smm")
        a2_sev = sem("a2_sev")
        a2_sst = [sem("a2_sst0"), sem("a2_sst1")]
        a2_kvu = sem("a2_kvu")
        f_dld = [[sem("f_dld%d_%d" % (i, j)) for j in range(2)] for i in range(2)]
        f_dls = [[sem("f_dls%d_%d" % (i, j)) for j in range(2)] for i in range(2)]
        f_deq = [sem("f_deq%d" % i) for i in range(2)]
        f_ldx = [sem("f_ldx%d" % i) for i in range(2)]
        f_ldz = sem("f_ldz")
        f_xc = [sem("f_xc%d" % i) for i in range(2)]
        f_lnr = [sem("f_lnr%d" % i) for i in range(2)]
        f_tr = [sem("f_tr%d" % i) for i in range(2)]
        f_tre = [sem("f_tre%d" % i) for i in range(2)]
        f_ymm = [sem("f_ymm%d" % i) for i in range(2)]
        f_yev = [sem("f_yev%d" % i) for i in range(2)]
        f_zmm = [sem("f_zmm%d" % i) for i in range(2)]
        f_zev = [sem("f_zev%d" % i) for i in range(2)]
        f_ysh = [sem("f_ysh%d" % i) for i in range(2)]
        f_gv = sem("f_gv")
        f_p5 = sem("f_p5")
        f_og0 = sem("f_og0")
        x2o = sem("x2o")
        a2_dkv = sem("a2_dkv")
        f_gs = sem("f_gs")
        f_st = [sem("f_st%d" % i) for i in range(2)]

        block = es.enter_context(nc.Block())

        @block.sync
        def _(sync):
            sync.dma_start(out=ident_s[:], in_=identf).then_inc(ldc, 16)
            sync.dma_start(out=maskw_s[:], in_=maskw).then_inc(ldc, 16)
            sync.dma_start(out=b1_s[:], in_=b1).then_inc(ldc, 16)
            for c in range(2):
                sync.dma_start(
                    out=ar(A1_XB[c % 2], 2048).rearrange("p (s d) -> p s d", s=2),
                    in_=x_in[c * C1:(c + 1) * C1, :].rearrange("(s p) d -> p s d", p=P)
                ).then_inc(a1_ldx[c % 2], 16)
            for c in range(NC1):
                sync.wait_ge(a1_qr, 2 * c + 1)
                sync.dma_start(out=qf_sc[c], in_=ar(A1_QF, 2048)).then_inc(st_q, 16)
                sync.wait_ge(a1_qr, 2 * c + 2)
                sync.dma_start(out=kf_sc[c], in_=ar(A1_KF, 2048)).then_inc(st_k, 16)
                sync.wait_ge(a1_ve, 4 * c + 4)
                inst = sync.dma_start(out=v_sc[c].rearrange("s p d -> p s d"),
                                      in_=ar(A1_V, 2048).rearrange("p (s d) -> p s d", s=2))
                inst.then_inc(st_v, 16)
                if c + 2 < NC1:
                    sync.wait_ge(a1_xc, c + 1)
                    sync.dma_start(
                        out=ar(A1_XB[c % 2], 2048).rearrange("p (s d) -> p s d", s=2),
                        in_=x_in[(c + 2) * C1:(c + 3) * C1, :]
                        .rearrange("(s p) d -> p s d", p=P)
                    ).then_inc(a1_ldx[c % 2], 16)
            sync.wait_ge(st_q, 16 * NC1)
            sync.wait_ge(st_k, 16 * NC1)
            sync.wait_ge(st_v, 16 * NC1)
            sync.nop().then_inc(phb[0], 1)
            sync.wait_ge(phb[0], PHB)
            for c in range(8 if MP >= 2 else 0):
                if c >= 2:
                    sync.wait_ge(a2_cons, c - 1)
                sl = c % 2
                for par in range(2):
                    sync.dma_start(
                        out=WA[0:64, A2_QF[sl]:A2_QF[sl] + 4096]
                        .rearrange("r (p two t) -> r p two t", two=2, t=256)
                        [:, :, par, :],
                        in_=qf_sc[c][par * 64:par * 64 + 64]
                        .rearrange("r (p t) -> r p t", p=8)
                    ).then_inc(a2_ld[sl], 16)
                    sync.dma_start(
                        out=WA[0:64, A2_KF[sl]:A2_KF[sl] + 4096]
                        .rearrange("r (p two t) -> r p two t", two=2, t=256)
                        [:, :, par, :],
                        in_=kf_sc[c][par * 64:par * 64 + 64]
                        .rearrange("r (p t) -> r p t", p=8)
                    ).then_inc(a2_ld[sl], 16)
                for sub in range(2):
                    sync.dma_start(
                        out=WA[:, A2_VA[sl] + sub * 1040:A2_VA[sl] + sub * 1040 + 1040]
                        .rearrange("r (h e) -> r h e", e=65)[:, :, 0:64],
                        in_=v_sc[c, sub].rearrange("p (h e) -> p h e", e=64)
                    ).then_inc(a2_ld[sl], 16)
            sync.wait_ge(a2_sst[0], 16 * 8 if MP >= 2 else 0)
            sync.wait_ge(a2_sst[1], 16 * 8 if MP >= 2 else 0)
            sync.nop().then_inc(phb[1], 1)
            for fi in range(max(0, MP - 2)):
                sync.wait_ge(phb[1 + fi], PHB)
                for c in range(NC1):
                    if c >= 1:
                        sync.wait_ge(f_xc[fi], c)
                    sync.dma_start(
                        out=ar(F_X, 2048).rearrange("p (s d) -> p s d", s=2),
                        in_=x_in[c * C1:(c + 1) * C1, :]
                        .rearrange("(s p) d -> p s d", p=P)
                    ).then_inc(f_ldx[fi], 16)
                    if fi == 1:
                        if c == 0:
                            sync.wait_ge(f_deq[1], 24)
                        if c >= 1:
                            sync.wait_ge(f_st[1], 32 * c)
                        sync.dma_start(
                            out=ar(F_S, 2048).rearrange("p (s d) -> p s d", s=2),
                            in_=s_sc[2 * c:2 * c + 2].rearrange("s p d -> p s d")
                        ).then_inc(f_ldz, 16)
                        sync.dma_start(
                            out=ar(F_ZP, 2048).rearrange("p (s d) -> p s d", s=2),
                            in_=z2p_sc[c].rearrange("s p d -> p s d")
                        ).then_inc(f_ldz, 16)
                    if fi == 0:
                        sync.wait_ge(f_zev[0], 2 * c + 2)
                        inst = sync.dma_start(
                            out=z2p_sc[c].rearrange("s p d -> p s d"),
                            in_=ar(F_ZP, 2048).rearrange("p (s d) -> p s d", s=2))
                        inst.then_inc(f_st[0], 16)
                if fi == 0:
                    sync.wait_ge(f_st[0], 16 * NC1)
                    sync.nop().then_inc(phb[2], 1)
            sync.wait_ge(f_st[1], 32 * NC1 if MP >= 4 else 0)

        @block.gpsimd
        def _(gpsimd):
            gpsimd.dma_start(out=grep[:], in_=lng.partition_broadcast(P)
                             ).then_inc(ldcg, 16)
            gpsimd.dma_start(out=brep[:], in_=lnb.partition_broadcast(P)
                             ).then_inc(ldcg, 16)
            gpsimd.dma_start(out=b2rep[:], in_=b2.partition_broadcast(P)
                             ).then_inc(ldcg, 16)
            gpsimd.dma_start(out=out, in_=x_in).then_inc(x2o, 16)
            gpsimd.engine_nop().then_inc(phb[0], 1)
            gpsimd.wait_ge(phb[0], PHB)
            for c in range(8 if MP >= 2 else 0):
                for ts_ in range(2):
                    gpsimd.wait_ge(a2_sev, 2 * c + ts_ + 1)
                    gpsimd.dma_start(out=s_sc[2 * c + ts_],
                                     in_=ar(A2_S[c % 2] + ts_ * 1024, 1024)
                                     ).then_inc(a2_sst[c % 2], 16)
            gpsimd.engine_nop().then_inc(phb[1], 1)
            gpsimd.wait_ge(phb[1], PHB if MP >= 2 else 1)
            for fi in range(max(0, MP - 2)):
                if fi == 1:
                    gpsimd.engine_nop().then_inc(phb[2], 1)
                    gpsimd.wait_ge(phb[2], PHB)
                for kt in range(8):
                    if kt >= 2:
                        gpsimd.wait_ge(f_deq[fi], kt - 1)
                    gpsimd.dma_start(
                        out=ar(F_DEQ[kt % 2], 2048),
                        in_=fc1q[kt * 128:(kt + 1) * 128, fi * 2048:(fi + 1) * 2048]
                    ).then_inc(f_dld[fi][kt % 2], 16)

                for i in range(16):
                    kt = fi * 16 + i
                    gpsimd.wait_ge(f_deq[fi], 7 + i)
                    gpsimd.dma_start(
                        out=ar(F_IC2[i % 2], 1024),
                        in_=fc2q[kt * 128:(kt + 1) * 128, :]
                    ).then_inc(f_dld[fi][i % 2], 16)

                if fi == 1:
                    gpsimd.wait_ge(x2o, 16)
                    for c in range(NC1):
                        gpsimd.wait_ge(f_ldz, 32 * (c + 1))
                        gpsimd.wait_ge(f_lnr[1], c + 1)
                        og2 = ar(F_ZP, 2048)
                        og3 = og2.rearrange("r (s d) -> r s d", d=1024)
                        gpsimd.tensor_add(og2, og2, ar(F_XN, 2048))
                        gpsimd.drain()
                        gpsimd.tensor_tensor(og3, og3,
                                             b2rep[:].unsqueeze(1).broadcast_to([P, 2, D]),
                                             ALU.add)
                        gpsimd.drain()
                        gpsimd.engine_nop().then_inc(f_og0, 1)
                        for s_ in range(2):
                            gpsimd.wait_ge(f_p5, 2 * c + s_ + 1)
                            og = ar(F_ZP + s_ * 1024, 1024)
                            gpsimd.tensor_add(og, og, ar(F_DEQ[0] + s_ * 1024, 1024))
                            gpsimd.drain()
                            gpsimd.engine_nop().then_inc(f_gv, 1)
                            gpsimd.wait_ge(f_gs, 2 * c + s_ + 1)
                            gpsimd.tensor_mul(og, og, ar(F_S + s_ * 1024, 1024))
                            gpsimd.drain()
                            gpsimd.dma_start(
                                out=out[c * C1 + s_ * P:c * C1 + (s_ + 1) * P, :],
                                in_=og, accum_op=ALU.add
                            ).then_inc(f_st[1], 16)

        @block.vector
        def _(vector):
            vd = DrainAfter(vector)
            vector.wait_ge(ldc, 48)
            vector.wait_ge(ldcg, 48)
            for g24 in range(24):
                wofsh, wofsl = [(WQTH, WQTL), (WKTH, WKTL), (WVTH, WVTL)][g24 // 8]
                et = g24 % 8
                for grp in range(2):
                    gi = 2 * g24 + grp
                    vector.wait_ge(a1_tr, gi + 1)
                    sel = lambda base: WB[:, base:base + 8192].rearrange(
                        "r (dt e) -> r dt e", dt=8)[
                        :, 4 * grp:4 * grp + 4, et * 128:et * 128 + 128]
                    psrc = bk(gi % 2).rearrange("r (j t) -> r j t", j=4)
                    vd.tensor_copy(sel(wofsh), psrc)
                    vector.tensor_tensor(sel(wofsl), psrc, sel(wofsh),
                                         ALU.subtract).then_inc(a1_wev, 1)
            for c in range(NC1):
                vector.wait_ge(a1_ldx[c % 2], 16 * (c // 2 + 1))
                xb = A1_XB[c % 2]
                x2d = ar(xb, 2048)
                x3d = x2d.rearrange("r (s d) -> r s d", d=1024)
                pers = stats[:, 8 * c:8 * c + 8].rearrange("r (s k) -> r s k", k=4)
                mcols = pers[:, :, 0:1]
                rcols = pers[:, :, 1:2]
                m2cols = pers[:, :, 2:3]
                r2cols = pers[:, :, 3:4]
                acols = stats[:, 64:66].unsqueeze(2)
                iacols = stats[:, 66:68].unsqueeze(2)
                ycols = stats[:, 68:70].unsqueeze(2)
                tcols = stats[:, 70:72].unsqueeze(2)
                tmp2d = ar(A1_TMP, 2048)
                tmp3d = tmp2d.rearrange("r (s d) -> r s d", d=1024)

                def stats_block(srcap2, srcap3, mo, ao):
                    vd.reduce_sum(mo, srcap3, axis=AX)
                    vd.tensor_scalar_mul(mo, mo, 1.0 / D)
                    vd.tensor_mul(tmp2d, srcap2, srcap2)
                    vd.reduce_sum(ao, tmp3d, axis=AX)
                    vd.tensor_scalar_mul(ao, ao, 1.0 / D)
                    vd.tensor_mul(tcols, mo, mo)
                    vd.tensor_sub(ao, ao, tcols)
                    vd.tensor_scalar_add(ao, ao, EPS_LN)
                    vd.reciprocal(iacols, ao)

                def polish(ro):
                    vd.tensor_mul(tcols, ycols, ycols)
                    vd.tensor_mul(tcols, tcols, acols)
                    vd.tensor_scalar(tcols, tcols, -0.5, 1.5, ALU.mult, ALU.add)
                    vd.tensor_mul(ro, tcols, ycols)

                stats_block(x2d, x3d, mcols, acols)
                vector.engine_nop().then_inc(a1_vr, 1)
                vector.wait_ge(a1_sq, 2 * c + 1)
                if c >= 1:
                    vector.wait_ge(a1_tr, 48 + 4 * c)
                polish(rcols)
                for s_ in range(2):
                    sc_ = stats[:, (8 * c + 4 * s_):(8 * c + 4 * s_) + 4]
                    vd.tensor_scalar(ar(A1_XN + s_ * 1024, 1024),
                                     ar(xb + s_ * 1024, 1024),
                                     sc_[:, 0:1], sc_[:, 1:2],
                                     ALU.subtract, ALU.mult)
                inst = vector.engine_nop()
                inst.then_inc(a1_xc, 1)
                vector.engine_nop().then_inc(a1_xnr, 1)
                xn2d = ar(A1_XN, 2048)
                xn3d = xn2d.rearrange("r (s d) -> r s d", d=1024)
                stats_block(xn2d, xn3d, m2cols, acols)
                vector.engine_nop().then_inc(a1_vr, 1)
                vector.wait_ge(a1_sq, 2 * c + 2)
                polish(r2cols)
                for g in range(4):
                    vector.wait_ge(a1_tr, 48 + 4 * c + g + 1)
                    if c >= 1 and g == 0:
                        vector.wait_ge(a1_vmm, 4 * c)
                    half, s_ = divmod(g, 2)
                    sel = lambda base: ARB[:, base:base + 2048].rearrange(
                        "r (dt t) -> r dt t", dt=8)[
                        :, 4 * half:4 * half + 4, s_ * 128:s_ * 128 + 128]
                    psrc = bk(g % 2).rearrange("r (j t) -> r j t", j=4)
                    vd.tensor_copy(sel(A1_XNTH), psrc)
                    vector.tensor_tensor(sel(A1_XNTL), psrc, sel(A1_XNTH),
                                         ALU.subtract).then_inc(a1_xev, 1)
            vector.engine_nop().then_inc(phb[0], 1)
            vector.wait_ge(phb[0], PHB)
            if MP >= 2:
                vd.memset(kvsb[0][:], 0.0)
            for sl in range(2 if MP >= 2 else 0):
                for sub in range(2):
                    vd.memset(WA[:, A2_VA[sl] + sub * 1040:A2_VA[sl] + sub * 1040 + 1040]
                              .rearrange("r (h e) -> r h e", e=65)[:, :, 64:65], 1.0)
            for et8 in range(8 if MP >= 2 else 0):
                for grp in range(2):
                    gi = 2 * et8 + grp
                    vector.wait_ge(a2_tr, gi + 1)
                    sel = lambda base: WB[:, base:base + 8192].rearrange(
                        "r (dt e) -> r dt e", dt=8)[
                        :, 4 * grp:4 * grp + 4, et8 * 128:et8 * 128 + 128]
                    psrc = bk(gi % 2).rearrange("r (j t) -> r j t", j=4)
                    vd.tensor_copy(sel(WOTH), psrc)
                    vector.tensor_tensor(sel(WOTL), psrc, sel(WOTH),
                                         ALU.subtract).then_inc(a2_wev, 1)
            for c in range(8 if MP >= 2 else 0):
                sl = c % 2
                for g in range(4):
                    vector.wait_ge(a2_tr, 16 + 4 * c + g + 1)
                    vector.tensor_copy(
                        WA[:, A2_KFT + g * 512:A2_KFT + g * 512 + 512],
                        bk(4 + g % 2)).then_inc(a2_ktre, 1)
                for h in range(16):
                    vector.wait_ge(a2_amm, 16 * c + h + 1)
                    vector.tensor_tensor(
                        ar(A2_A + h * 384, 384),
                        bk(2 + h % 2, 0, 384),
                        maskw_s[:],
                        ALU.mult).then_inc(a2_aev, 1)
                vector.wait_ge(a2_dkv, c + 1)
                ksrc, kdst = kvsb[c % 2], kvsb[(c + 1) % 2]
                vector.tensor_add(kdst[:, 0:455], ksrc[:, 0:455], PS[0:64, 0:455])
                vector.tensor_add(kdst[:, 455:910], ksrc[:, 455:910], PS[0:64, 512:967])
                vd.tensor_add(kdst[:, 910:1040], ksrc[:, 910:1040], PS[0:64, 3584:3714])
                vector.engine_nop().then_inc(a2_kvu, 1)
                for ts_ in range(2):
                    vector.wait_ge(a2_nmm, 2 * c + ts_ + 1)
                    den = AR[:, A2_DEN:A2_DEN + 16]
                    rden = AR[:, A2_DEN + 16:A2_DEN + 32]
                    for bi, (h0, nh) in enumerate([(0, 7), (7, 7), (14, 2)]):
                        vector.tensor_copy(
                            den[:, h0:h0 + nh].unsqueeze(2),
                            ps_strided(NUM_PS + 512 * bi + 64, 65, nh, 1))
                    vector.drain()
                    vd.tensor_scalar_add(den[:], den[:], EPS_DEN)
                    vd.reciprocal(rden[:], den[:])
                    for bi, (h0, nh) in enumerate([(0, 7), (7, 7), (14, 2)]):
                        vector.tensor_tensor(
                            ar(A2_NUMR, 1024).rearrange("r (h e) -> r h e", e=64)
                            [:, h0:h0 + nh, :],
                            ps_strided(NUM_PS + 512 * bi, 65, nh, 64),
                            rden[:, h0:h0 + nh].unsqueeze(2).broadcast_to([P, nh, 64]),
                            ALU.mult)
                    vector.engine_nop().then_inc(a2_nev, 1)
                    for g in range(2):
                        vector.wait_ge(a2_ntr, 4 * c + 2 * ts_ + g + 1)
                        vd.tensor_copy(arb(2 * A2_NUMRT + g * 512, 512), bk(2 + g))
                        vector.tensor_tensor(arb(2 * A2_NUMRT + 1024 + g * 512, 512),
                                             bk(2 + g),
                                             arb(2 * A2_NUMRT + g * 512, 512),
                                             ALU.subtract).then_inc(a2_ntre, 1)
            vector.engine_nop().then_inc(phb[1], 1)
            for fi in range(max(0, MP - 2)):
                vector.wait_ge(phb[1 + fi], PHB)
                if fi == 0:
                    vd.tensor_scalar_mul(b2rep[:], b2rep[:], 0.5)
                for ti in range(8):
                    is1 = ti < 8
                    n = 2048 if is1 else 1024
                    slot = (ti % 2) if is1 else ((ti - 8) % 2)
                    vector.wait_ge(f_dld[fi][ti % 2], 16 * (ti // 2 + 1))
                    vector.wait_ge(f_dls[fi][ti % 2], 16 * (ti // 2 + 1))
                    if is1:
                        dh = wb(W1AH + ti * 2048, n)
                        dl = wb(W1AL + ti * 2048, n)
                    else:
                        dh = wb(W2AH + (ti - 8) * 1024, n)
                        dl = wb(W2AL + (ti - 8) * 1024, n)
                    tmp = ar(F_XN, n)
                    vd.scalar_tensor_tensor(
                        tmp, ar(F_DEQ[slot], n), -128.0, ar(F_SREP[slot], n),
                        ALU.add, ALU.mult)
                    vd.tensor_copy(dh, tmp)
                    vd.tensor_sub(dl, tmp, dh).then_inc(f_deq[fi], 1)
                for c in range(NC1):
                    vector.wait_ge(f_ldx[fi], 16 * (c + 1))
                    if c >= 1:
                        vector.wait_ge(f_tr[fi], 4 * c)
                        if fi == 0:
                            vector.wait_ge(f_st[0], 16 * c)
                        else:
                            vector.wait_ge(f_og0, c)
                    for s_ in range(2):
                        sc_ = stats[:, (2 * c + s_) * 4:(2 * c + s_) * 4 + 4]
                        xap = ar(F_X + s_ * 1024, 1024)
                        xnap = ar(F_XN + s_ * 1024, 1024)
                        vd.tensor_scalar(xnap, xap, sc_[:, 0:1], sc_[:, 1:2],
                                         ALU.subtract, ALU.mult)
                        vd.tensor_scalar(ar(F_LN2 + s_ * 1024, 1024), xnap,
                                         sc_[:, 2:3], sc_[:, 3:4],
                                         ALU.subtract, ALU.mult)
                    gb = ar(F_LN2, 2048).rearrange("r (s d) -> r s d", d=1024)
                    vd.tensor_tensor(gb, gb,
                                     grep[:].unsqueeze(1).broadcast_to([P, 2, D]),
                                     ALU.mult)
                    vd.tensor_tensor(gb, gb,
                                     brep[:].unsqueeze(1).broadcast_to([P, 2, D]),
                                     ALU.add)
                    vector.engine_nop().then_inc(f_lnr[fi], 1)
                    vector.engine_nop().then_inc(f_xc[fi], 1)
                    for g in range(4):
                        vector.wait_ge(f_tr[fi], 4 * c + g + 1)
                        half, s_ = divmod(g, 2)
                        sel = lambda base: ARB[:, base:base + 2048].rearrange(
                            "r (dt t) -> r dt t", dt=8)[
                            :, 4 * half:4 * half + 4, s_ * 128:s_ * 128 + 128]
                        psrc = bk(g % 2).rearrange("r (j t) -> r j t", j=4)
                        vd.tensor_copy(sel(F_LN2TH), psrc)
                        vector.tensor_tensor(sel(F_LN2TL), psrc, sel(F_LN2TH),
                                             ALU.subtract).then_inc(f_tre[fi], 1)
                    if c == 0:
                        for ti in range(8, 24):
                            n = 1024
                            slot = (ti - 8) % 2
                            vector.wait_ge(f_dld[fi][ti % 2], 16 * (ti // 2 + 1))
                            vector.wait_ge(f_dls[fi][ti % 2], 16 * (ti // 2 + 1))
                            dh = wb(W2AH + (ti - 8) * 1024, n)
                            dl = wb(W2AL + (ti - 8) * 1024, n)
                            stg = ar(F_IC2[slot], n)
                            vd.scalar_tensor_tensor(
                                stg, stg, -128.0, ar(F_SR2[slot], n),
                                ALU.add, ALU.mult)
                            vd.tensor_copy(dh, stg)
                            vd.tensor_sub(dl, stg, dh).then_inc(f_deq[fi], 1)
                    for ft in range(16):
                        vector.wait_ge(f_ysh[fi], 16 * c + ft + 1)
                        vector.tensor_sub(
                            arb(F_YL + ft * 256, 256),
                            YTMP[:, (ft % 2) * 256:(ft % 2) * 256 + 256],
                            arb(F_YH + ft * 256, 256)).then_inc(f_yev[fi], 1)

                if fi == 0:
                    vector.engine_nop().then_inc(phb[2], 1)

        @block.scalar
        def _(scalar):
            scalar.wait_ge(ldc, 48)
            scalar.wait_ge(ldcg, 48)
            for widx, wsrc in enumerate([wq, wk, wv]):
                for et in range(8):
                    g = widx * 8 + et
                    if g >= 2:
                        scalar.wait_ge(a1_tr, 2 * g - 2)
                    scalar.dma_start(out=WA[:, A1_WST[g % 2]:A1_WST[g % 2] + 1024],
                                     in_=wsrc[et * 128:(et + 1) * 128, :]
                                     ).then_inc(a1_wld[g % 2], 16)
            for c in range(NC1):
                for rnd in range(2):
                    scalar.wait_ge(a1_vr, 2 * c + rnd + 1)
                    scalar.activation(stats[:, 68:70], stats[:, 66:68],
                                      AF.Sqrt).then_inc(a1_sq, 1)
                for nt in range(8):
                    qi = 8 * c + nt
                    scalar.wait_ge(a1_qmq, qi + 1)
                    if nt == 0 and c >= 1:
                        scalar.wait_ge(st_q, 16 * c)
                    scalar.activation(ar(A1_QF, 2048)[:, nt * 256:nt * 256 + 256],
                                      bk(2 + nt % 2, 0, 256), AF.Exp
                                      ).then_inc(a1_qeq, 1)
                scalar.drain()
                scalar.activation(ar(A1_QF, 2048), ar(A1_QF, 2048), AF.Ln, bias=1.0)
                scalar.drain()
                scalar.activation(ar(A1_QF, 2048), ar(A1_QF, 2048), AF.Copy,
                                  bias=EPS_PHI).then_inc(a1_qr, 1)
                for nt in range(8):
                    qi = 8 * c + nt
                    scalar.wait_ge(a1_qmk, qi + 1)
                    if nt == 0 and c >= 1:
                        scalar.wait_ge(st_k, 16 * c)
                    scalar.activation(ar(A1_KF, 2048)[:, nt * 256:nt * 256 + 256],
                                      bk(4 + nt % 2, 0, 256), AF.Exp
                                      ).then_inc(a1_qek, 1)
                scalar.drain()
                scalar.activation(ar(A1_KF, 2048), ar(A1_KF, 2048), AF.Ln, bias=1.0)
                scalar.drain()
                scalar.activation(ar(A1_KF, 2048), ar(A1_KF, 2048), AF.Copy,
                                  bias=EPS_PHI).then_inc(a1_qr, 1)
                for vi4 in range(4):
                    s_, jt = divmod(vi4, 2)
                    vi = 4 * c + vi4
                    scalar.wait_ge(a1_vmm, vi + 1)
                    if vi4 == 0 and c >= 1:
                        scalar.wait_ge(st_v, 16 * c)
                    scalar.copy(
                        ar(A1_V, 2048)[:, s_ * 1024 + jt * 512:s_ * 1024 + jt * 512 + 512],
                        bk(6 + jt, 0, 512)).then_inc(a1_ve, 1)
                    if c == NC1 - 1 and vi4 == 3:
                        scalar.nop().then_inc(phb[0], 1)
            scalar.wait_ge(phb[0], PHB)
            for et in range(8 if MP >= 2 else 0):
                if et >= 1:
                    scalar.wait_ge(a2_tr, 2 * et)
                scalar.dma_start(out=WA[:, A2_WST[et % 2]:A2_WST[et % 2] + 1024],
                                 in_=wo[et * 128:(et + 1) * 128, :]
                                 ).then_inc(a2_wld[et % 2], 16)
            for c in range(8 if MP >= 2 else 0):
                sl = c % 2
                for ts_ in range(2):
                    scalar.wait_ge(a2_smm, 2 * c + ts_ + 1)
                    if c >= 2 and ts_ == 0:
                        scalar.wait_ge(a2_sst[sl], 32 * (c // 2))
                    scalar.copy(ar(A2_S[sl] + ts_ * 1024, 512), bk(4, 0, 512))
                    scalar.copy(ar(A2_S[sl] + ts_ * 1024 + 512, 512), bk(5, 0, 512)
                                ).then_inc(a2_sev, 1)
                if c == 7:
                    scalar.nop().then_inc(phb[1], 1)
            for fi in range(max(0, MP - 2)):
                scalar.wait_ge(phb[1 + fi], PHB)
                for ti in range(24):
                    is1 = ti < 8
                    if ti >= 2:
                        scalar.wait_ge(f_deq[fi], ti - 1)
                    if is1:
                        scalar.dma_start(
                            out=ar(F_SREP[ti % 2], 2048),
                            in_=s1[ti:ti + 1, fi * 2048:(fi + 1) * 2048]
                            .partition_broadcast(P)
                        ).then_inc(f_dls[fi][ti % 2], 16)
                    else:
                        kt = fi * 16 + (ti - 8)
                        if ti == 8:
                            scalar.wait_ge(f_tr[fi], 4)
                        scalar.dma_start(
                            out=ar(F_SR2[ti % 2], 1024),
                            in_=s2[kt:kt + 1, :].partition_broadcast(P)
                        ).then_inc(f_dls[fi][ti % 2], 16)
                for c in range(NC1):
                    for ft in range(16):
                        yi = 16 * c + ft
                        scalar.wait_ge(f_ymm[fi], yi + 1)
                        if ft == 0 and c >= 1:
                            scalar.wait_ge(f_zmm[fi], 2 * c)
                        if yi >= 2:
                            scalar.wait_ge(f_yev[fi], yi - 1)
                        scalar.activation(
                            YTMP[:, (ft % 2) * 256:(ft % 2) * 256 + 256],
                            bk(2 + ft % 2, 0, 256), AF.Silu,
                            bias=b1_s[:, fi * 16 + ft:fi * 16 + ft + 1])
                        scalar.drain()
                        scalar.copy(arb(F_YH + ft * 256, 256),
                                    YTMP[:, (ft % 2) * 256:(ft % 2) * 256 + 256]
                                    ).then_inc(f_ysh[fi], 1)
                        if fi == 0 and c == NC1 - 1 and ft == 15:
                            scalar.nop().then_inc(phb[2], 1)
                    if fi == 0:
                        for s_ in range(2):
                            scalar.wait_ge(f_zmm[0], 2 * c + s_ + 1)
                            if s_ == 0 and c >= 1:
                                scalar.wait_ge(f_st[0], 16 * c)
                            scalar.activation(
                                ar(F_ZP + s_ * 1024, 1024),
                                PS[:, (4 + 2 * s_) * 512:(4 + 2 * s_) * 512 + 1024],
                                AF.Copy, scale=0.5).then_inc(f_zev[0], 1)
                    else:
                        for s_ in range(2):
                            scalar.wait_ge(f_zmm[1], 2 * c + s_ + 1)
                            if c >= 1 and s_ == 0:
                                scalar.wait_ge(f_gv, 2 * c)
                            scalar.activation(
                                ar(F_DEQ[0] + s_ * 1024, 1024),
                                PS[:, (4 + 2 * s_) * 512:(4 + 2 * s_) * 512 + 1024],
                                AF.Copy, scale=0.5).then_inc(f_p5, 1)
                        for s_ in range(2):
                            scalar.wait_ge(f_gv, 2 * c + s_ + 1)
                            scalar.activation(ar(F_ZP + s_ * 1024, 1024),
                                              ar(F_ZP + s_ * 1024, 1024), AF.Silu
                                              ).then_inc(f_gs, 1)

        @block.tensor
        def _(tensor):
            tensor.wait_ge(ldc, 48)
            tensor.wait_ge(ldcg, 48)
            for g in range(24):
                tensor.wait_ge(a1_wld[g % 2], 16 * (g // 2 + 1))
                for grp in range(2):
                    gi = 2 * g + grp
                    if gi >= 2:
                        tensor.wait_ge(a1_wev, gi - 1)
                    for j in range(4):
                        dt = 4 * grp + j
                        inst = tensor.transpose(
                            bk(gi % 2, j * 128, 128),
                            WA[:, A1_WST[g % 2] + dt * 128:A1_WST[g % 2] + dt * 128 + 128],
                            ident_s[:])
                        if j == 3:
                            inst.then_inc(a1_tr, 1)
            for c in range(NC1):
                tensor.wait_ge(a1_xnr, c + 1)
                for g in range(4):
                    li = 4 * c + g
                    if li >= 2:
                        tensor.wait_ge(a1_xev, li - 1)
                    elif li < 2:
                        tensor.wait_ge(a1_wev, 47 + (g % 2))
                    half, s_ = divmod(g, 2)
                    for j in range(4):
                        dt = 4 * half + j
                        inst = tensor.transpose(
                            bk(g % 2, j * 128, 128),
                            ar(A1_XN + s_ * 1024, 1024)[:, dt * 128:dt * 128 + 128],
                            ident_s[:])
                        if j == 3:
                            inst.then_inc(a1_tr, 1)
                if c == 0:
                    tensor.wait_ge(a1_wev, 48)
                tensor.wait_ge(a1_xev, 4 * c + 4)
                for nt in range(8):
                    qi = 8 * c + nt
                    if qi >= 2:
                        tensor.wait_ge(a1_qeq, qi - 1)
                    for dt in range(8):
                        inst = mm3(tensor, bk(2 + nt % 2, 0, 256),
                                   wb(WQTH + dt * 1024 + nt * 128, 128),
                                   wb(WQTL + dt * 1024 + nt * 128, 128),
                                   arb(A1_XNTH + dt * 256, 256),
                                   arb(A1_XNTL + dt * 256, 256),
                                   start=(dt == 0), stop=(dt == 7))
                        if dt == 7:
                            inst.then_inc(a1_qmq, 1)
                for nt in range(8):
                    qi = 8 * c + nt
                    if qi >= 2:
                        tensor.wait_ge(a1_qek, qi - 1)
                    for dt in range(8):
                        inst = mm3(tensor, bk(4 + nt % 2, 0, 256),
                                   wb(WKTH + dt * 1024 + nt * 128, 128),
                                   wb(WKTL + dt * 1024 + nt * 128, 128),
                                   arb(A1_XNTH + dt * 256, 256),
                                   arb(A1_XNTL + dt * 256, 256),
                                   start=(dt == 0), stop=(dt == 7))
                        if dt == 7:
                            inst.then_inc(a1_qmk, 1)
                for vi4 in range(4):
                    s_, jt = divmod(vi4, 2)
                    vi = 4 * c + vi4
                    if vi >= 2:
                        tensor.wait_ge(a1_ve, vi - 1)
                    for dt in range(8):
                        inst = mm3(tensor, bk(6 + jt, 0, 512),
                                   arb(A1_XNTH + dt * 256 + s_ * 128, 128),
                                   arb(A1_XNTL + dt * 256 + s_ * 128, 128),
                                   wb(WVTH + dt * 1024 + jt * 512, 512),
                                   wb(WVTL + dt * 1024 + jt * 512, 512),
                                   start=(dt == 0), stop=(dt == 7))
                        if dt == 7:
                            inst.then_inc(a1_vmm, 1)
                if c == NC1 - 1 and vi4 == 3:
                    tensor.nop().then_inc(phb[0], 1)
            tensor.wait_ge(phb[0], PHB)
            for et in range(8 if MP >= 2 else 0):
                tensor.wait_ge(a2_wld[et % 2], 16 * (et // 2 + 1))
                for grp in range(2):
                    gi = 2 * et + grp
                    if gi >= 2:
                        tensor.wait_ge(a2_wev, gi - 1)
                    for j in range(4):
                        dt = 4 * grp + j
                        inst = tensor.transpose(
                            bk(gi % 2, j * 128, 128),
                            WA[:, A2_WST[et % 2] + dt * 128:A2_WST[et % 2] + dt * 128 + 128],
                            ident_s[:])
                        if j == 3:
                            inst.then_inc(a2_tr, 1)
            for c in range(8 if MP >= 2 else 0):
                sl = c % 2
                tensor.wait_ge(a2_ld[sl], 96 * (c // 2 + 1))
                if c >= 1:
                    tensor.wait_ge(a2_sev, 2 * c)
                for g in range(4):
                    if 4 * c + g >= 2:
                        tensor.wait_ge(a2_ktre, 4 * c + g - 1)
                    for j in range(8):
                        f = 8 * g + j
                        h, sub = f // 2, f % 2
                        inst = tensor.transpose(
                            bk(4 + g % 2, j * 64, 64),
                            WA[0:64, A2_KF[sl] + h * 256 + sub * 128:
                               A2_KF[sl] + h * 256 + sub * 128 + 128],
                            ident_s[0:64, 0:64])
                        if j == 7:
                            inst.then_inc(a2_tr, 1)
                for h in range(16):
                    ai = 16 * c + h
                    if ai >= 2:
                        tensor.wait_ge(a2_aev, ai - 1)
                    if c >= 1 and h < 2:
                        tensor.wait_ge(a2_ntre, 4 * c)
                    qcol = A2_QF[sl] + h * 256
                    kcol = A2_KF[sl] + h * 256
                    tensor.matmul(
                        bk(2 + h % 2, 0, 256),
                        WA[0:64, kcol:kcol + 128],
                        WA[0:64, qcol:qcol + 256],
                        start=True, stop=True)
                    inst = tensor.matmul(
                        bk(2 + h % 2, 256, 128),
                        WA[0:64, kcol + 128:kcol + 256],
                        WA[0:64, qcol + 128:qcol + 256],
                        start=True, stop=True)
                    inst.then_inc(a2_amm, 1)
                tensor.wait_ge(a2_ktre, 4 * c + 4)
                if c >= 1:
                    tensor.wait_ge(a2_kvu, c)
                if c == 0:
                    tensor.wait_ge(a2_wev, 16)
                for h in range(16):
                    col = dkv_col(h)
                    for sub in range(2):
                        inst = tensor.matmul(
                            PS[0:64, col:col + 65],
                            WA[:, A2_KFT + h * 128 + sub * 64:
                               A2_KFT + h * 128 + sub * 64 + 64],
                            WA[:, A2_VA[sl] + sub * 1040 + h * 65:
                               A2_VA[sl] + sub * 1040 + h * 65 + 65],
                            start=(sub == 0), stop=(sub == 1))
                        if h == 15 and sub == 1:
                            inst.then_inc(a2_dkv, 1)
                for ts_ in range(2):
                    if ts_ == 0:
                        tensor.wait_ge(a2_ktre, 4 * c + 4)
                        tensor.wait_ge(a2_aev, 16 * c + 16)
                    else:
                        tensor.wait_ge(a2_sev, 2 * c + 1)
                    last = None
                    for h in range(16):
                        reg = PS[:, NUM_PS + num_off(h):NUM_PS + num_off(h) + 65]
                        va_h = lambda sub: WA[:, A2_VA[sl] + sub * 1040 + h * 65:
                                              A2_VA[sl] + sub * 1040 + h * 65 + 65]
                        if ts_ == 0:
                            last = tensor.matmul(reg, ar(A2_A + h * 384, 128), va_h(0),
                                                 start=True, stop=(c == 0))
                        else:
                            tensor.matmul(reg, ar(A2_A + h * 384 + 128, 128), va_h(0),
                                          start=True, stop=False)
                            last = tensor.matmul(reg, ar(A2_A + h * 384 + 256, 128),
                                                 va_h(1), start=False, stop=(c == 0))
                        if c > 0:
                            qcol = A2_QF[sl] + h * 256 + ts_ * 128
                            last = tensor.matmul(reg, WA[0:64, qcol:qcol + 128],
                                                 kvsb[c % 2][:, h * 65:h * 65 + 65],
                                                 start=False, stop=True)
                    last.then_inc(a2_nmm, 1)
                    tensor.wait_ge(a2_nev, 2 * c + ts_ + 1)
                    for g in range(2):
                        for j in range(4):
                            dt = 4 * g + j
                            inst = tensor.transpose(
                                bk(2 + g, j * 128, 128),
                                ar(A2_NUMR, 1024)[:, dt * 128:dt * 128 + 128],
                                ident_s[:])
                            if j == 3:
                                inst.then_inc(a2_ntr, 1)
                    tensor.wait_ge(a2_ntre, 4 * c + 2 * ts_ + 2)
                    for jt in range(2):
                        for nt in range(8):
                            inst = mm3(tensor, bk(4 + jt, 0, 512),
                                       arb(2 * A2_NUMRT + nt * 128, 128),
                                       arb(2 * A2_NUMRT + 1024 + nt * 128, 128),
                                       wb(WOTH + nt * 1024 + jt * 512, 512),
                                       wb(WOTL + nt * 1024 + jt * 512, 512),
                                       start=(nt == 0), stop=(nt == 7))
                            if jt == 1 and nt == 7:
                                inst.then_inc(a2_smm, 1)
                tensor.nop().then_inc(a2_cons, 1)
                if c == 7:
                    tensor.nop().then_inc(phb[1], 1)
            for fi in range(max(0, MP - 2)):
                tensor.wait_ge(phb[1 + fi], PHB)
                for c in range(NC1):
                    tensor.wait_ge(f_lnr[fi], c + 1)
                    if c >= 1:
                        if fi == 1:
                            tensor.wait_ge(f_p5, 2 * c)
                        else:
                            tensor.wait_ge(f_zev[0], 2 * c)
                    for g in range(4):
                        gi = 4 * c + g
                        if gi >= 2:
                            tensor.wait_ge(f_tre[fi], gi - 1)
                        half, s_ = divmod(g, 2)
                        for j in range(4):
                            dt = 4 * half + j
                            inst = tensor.transpose(
                                bk(g % 2, j * 128, 128),
                                ar(F_LN2 + s_ * 1024, 1024)[:, dt * 128:dt * 128 + 128],
                                ident_s[:])
                            if j == 3:
                                inst.then_inc(f_tr[fi], 1)
                    tensor.wait_ge(f_tre[fi], 4 * c + 4)
                    for ft in range(17):
                        if ft < 16:
                            yi = 16 * c + ft
                            if yi >= 2:
                                tensor.wait_ge(f_ysh[fi], yi - 1)
                            for kt in range(8):
                                if c == 0 and ft == 0:
                                    tensor.wait_ge(f_deq[fi], kt + 1)
                                inst = mm3(tensor, bk(2 + ft % 2, 0, 256),
                                           wb(W1AH + kt * 2048 + ft * 128, 128),
                                           wb(W1AL + kt * 2048 + ft * 128, 128),
                                           arb(F_LN2TH + kt * 256, 256),
                                           arb(F_LN2TL + kt * 256, 256),
                                           start=(kt == 0), stop=(kt == 7))
                                if kt == 7:
                                    inst.then_inc(f_ymm[fi], 1)
                        if ft >= 1:
                            zt = ft - 1
                            if c == 0:
                                tensor.wait_ge(f_deq[fi], 9 + zt)
                            tensor.wait_ge(f_yev[fi], 16 * c + zt + 1)
                            for s_ in range(2):
                                for jt in range(2):
                                    inst = mm3(tensor, bk(4 + 2 * s_ + jt, 0, 512),
                                               arb(F_YH + zt * 256 + s_ * 128, 128),
                                               arb(F_YL + zt * 256 + s_ * 128, 128),
                                               wb(W2AH + zt * 1024 + jt * 512, 512),
                                               wb(W2AL + zt * 1024 + jt * 512, 512),
                                               start=(zt == 0), stop=(zt == 15))
                                    if zt == 15 and jt == 1:
                                        inst.then_inc(f_zmm[fi], 1)
                if fi == 0 and c == NC1 - 1:
                    tensor.nop().then_inc(phb[2], 1)

    return nc


def _maskw():
    import ml_dtypes
    tril = (np.arange(P)[:, None] <= np.arange(P)[None, :])
    m = np.concatenate([tril, np.ones((P, P), bool), tril], axis=1)
    return m.astype(ml_dtypes.bfloat16)


_BUILT = {}


def kernel(L0_attn_x, L0_attn_Wq, L0_attn_Wk, L0_attn_Wv, L0_attn_Wo,
           L0_ffn_ln_gamma, L0_ffn_ln_beta, L0_ffn_fc1_q, L0_ffn_fc1_scales,
           L0_ffn_fc1_bias, L0_ffn_fc2_q, L0_ffn_fc2_scales, L0_ffn_fc2_bias):
    x = np.ascontiguousarray(np.asarray(L0_attn_x, dtype=np.float32))
    f32 = lambda a: np.ascontiguousarray(np.asarray(a, dtype=np.float32))
    i32 = lambda a: np.ascontiguousarray(np.asarray(a, dtype=np.int32))
    common = {
        "wq": f32(L0_attn_Wq), "wk": f32(L0_attn_Wk),
        "wv": f32(L0_attn_Wv), "wo": f32(L0_attn_Wo),
        "fc1q": i32(L0_ffn_fc1_q), "fc2q": i32(L0_ffn_fc2_q),
        "s1": f32(L0_ffn_fc1_scales), "s2": f32(L0_ffn_fc2_scales),
        "b1": np.ascontiguousarray(f32(L0_ffn_fc1_bias).reshape(32, P).T),
        "b2": f32(L0_ffn_fc2_bias).reshape(1, D),
        "lng": f32(L0_ffn_ln_gamma).reshape(1, D),
        "lnb": f32(L0_ffn_ln_beta).reshape(1, D),
        "identf": np.eye(P, dtype=np.float32),
        "maskw": _maskw(),
    }
    if "nc" not in _BUILT:
        _BUILT["nc"] = build_nc()
    nc = _BUILT["nc"]
    in_maps = [dict(common, x_in=x[b]) for b in range(x.shape[0])]
    res = run_bass_kernel_spmd(nc, in_maps, core_ids=list(range(len(in_maps))))
    return np.stack([r["out"] for r in res.results], axis=0)
